# version 04 (final)
# baseline (speedup 1.0000x reference)
import numpy as np
import ml_dtypes
import concourse.bass as bass
import concourse.mybir as mybir
from concourse.bass_utils import run_bass_kernel_spmd

F32 = mybir.dt.float32
BF16 = mybir.dt.bfloat16
AF = mybir.ActivationFunctionType
ALU = mybir.AluOpType
AX = mybir.AxisListType
BFnp = ml_dtypes.bfloat16

NCORES = 8
TOK = 2048
NT = 16
NSEQ = 4096
D = 1024
DCH = 8
H0 = 8
H1 = 16
MAGIC = 12582912.0
QMAX = 32767.0
LAM_INIT = 0.8
SC0 = float(1.0 / np.sqrt(128.0))
SC1 = 0.125
INTER = 3072
ICH = 24
MINTER = 8192
RMS_EPS1 = 1.1920928955078125e-07


class Sem:
    def __init__(self, nc, name):
        self.h = nc.alloc_semaphore(name)
        self.n = 0

    def inc(self, instr, k=1):
        instr.then_inc(self.h, k)
        self.n += k
        return self.n


class KB:
    def __init__(self, debug_taps=()):
        self.taps = set(debug_taps)
        nc = self.nc = bass.Bass("TRN2", target_bir_lowering=False, debug=False)
        self.in_ = {}
        for name, shape, dt in [
            ("x_own", [TOK, D], F32),
            ("wq", [D, 2048], F32), ("wk", [D, 2048], F32), ("wv", [D, 2048], F32),
            ("wo", [2048, D], F32),
            ("lq1", [8, 128], F32), ("lk1", [8, 128], F32),
            ("lq2", [8, 128], F32), ("lk2", [8, 128], F32),
            ("rms_scale", [256], F32),
            ("wgate", [INTER, D], F32), ("wup", [INTER, D], F32), ("wdown", [D, INTER], F32),
            ("wqkv", [3 * D, D], F32), ("wproj", [D, D], F32),
            ("w1", [D, MINTER], F32), ("w2", [MINTER, D], F32),
            ("expw", [512, 2], F32),
            ("masks", [128, 8 * 4 * 256], BF16),
            ("moe_gate", [128, 1], F32),
        ]:
            self.in_[name] = nc.dram_tensor(name, shape, dt, kind="ExternalInput").ap()
        self.out = nc.dram_tensor("out", [NT, 128, D], F32, kind="ExternalOutput").ap()

        self.dram = {}
        for name, shape, dt in [
            ("cc_ln0_src", [128, DCH * TOK], BF16),
            ("cc_ln0_dst", [NCORES, 128, DCH * TOK], BF16),
            ("cc_ln1_src", [128, DCH * TOK], BF16),
            ("cc_ln1_dst", [NCORES, 128, DCH * TOK], BF16),
            ("cc_mx_src", [1, 16], F32),
            ("cc_mx_dst", [1, 16], F32),
            ("cc_mln_src", [128, DCH * TOK], BF16),
            ("cc_mln_dst", [NCORES, 128, DCH * TOK], BF16),
            ("cc_mret_src", [512, D], F32),
            ("cc_mret_dst", [NCORES, 512, D], F32),
            ("sp_up", [ICH, 128, TOK], F32),
            ("sp_gate", [ICH, 128, TOK], F32),
            ("sp_t", [ICH, 128, TOK], F32),
            ("resid_hbm", [NT, 128, D], F32),
            ("gt_hbm", [16, 128, TOK], BF16),
            ("y2t_hbm", [8, 128, TOK], BF16),
        ]:
            self.dram[name] = nc.dram_tensor(name, shape, dt, kind="Internal")

        self.sD = Sem(nc, "sD")
        self.sG = Sem(nc, "sG")
        self.sC = Sem(nc, "sC")
        self.sT = Sem(nc, "sT")
        self.sV = Sem(nc, "sV")
        self.sS = Sem(nc, "sS")
        self.ALL = (self.sD, self.sG, self.sC, self.sT, self.sV, self.sS)
        self.pb = [nc.alloc_psum_tensor(f"pb{i}", [128, 512], F32) for i in range(8)]

    def wait_all(self, eng):
        for s in self.ALL:
            if s.n:
                eng.wait_ge(s.h, s.n)

    def barrier(self):
        nc = self.nc
        for eng in (nc.sync, nc.gpsimd, nc.tensor, nc.vector, nc.scalar):
            self.wait_all(eng)

    def vs(self, i):
        n = self.sV.inc(i)
        self.nc.vector.wait_ge(self.sV.h, n)
        return n

    def ss(self, i):
        n = self.sS.inc(i)
        self.nc.scalar.wait_ge(self.sS.h, n)
        return n

    def dmaS(self, out, in_):
        i = self.nc.sync.dma_start(out=out, in_=in_)
        self.sD.inc(i, 16)
        return self.sD.n

    def dmaG(self, out, in_):
        i = self.nc.gpsimd.dma_start(out=out, in_=in_)
        self.sG.inc(i, 16)
        return self.sG.n

    def tap(self, name, ap, shape=None, dt=None):
        if name not in self.taps:
            return
        shape = shape or list(ap.shape)
        t = self.nc.dram_tensor("tap_" + name, shape, dt or ap.dtype, kind="ExternalOutput").ap()
        self.barrier()
        self.dmaS(t[...], ap)
        self.nc.sync.wait_ge(self.sD.h, self.sD.n)

    def bcast128(self, dst_col, src11):
        nc = self.nc
        t, v = nc.tensor, nc.vector
        t.wait_ge(self.sV.h, self.sV.n)
        i = t.matmul(self.pb[7].ap()[:, 0:1], self.ones_f[0:1, 0:128], src11,
                     start=True, stop=True)
        self.sT.inc(i)
        v.wait_ge(self.sT.h, self.sT.n)
        self.vs(v.tensor_copy(dst_col, self.pb[7].ap()[:, 0:1]))

    def part_max(self, dst11, src_col):
        nc = self.nc
        t, v = nc.tensor, nc.vector
        t.wait_ge(self.sV.h, self.sV.n)
        i = t.transpose(self.pb[7].ap()[0:1, 0:128], src_col, self.ident_f[:, :])
        self.sT.inc(i)
        v.wait_ge(self.sT.h, self.sT.n)
        self.vs(v.tensor_reduce(dst11, self.pb[7].ap()[0:1, 0:128], axis=AX.X, op=ALU.max))

    def allreduce_max(self, idx):
        nc = self.nc
        g = nc.gpsimd
        self.barrier()
        gn = self.dmaG(self.dram["cc_mx_src"].ap()[0:1, :], self.mxA[0:1, :])
        g.wait_ge(self.sG.h, gn)
        i = g.collective_compute("AllReduce", ALU.max, replica_groups=[list(range(NCORES))],
                                 ins=[self.dram["cc_mx_src"].ap().opt()],
                                 outs=[self.dram["cc_mx_dst"].ap().opt()])
        self.sC.inc(i)
        g.wait_ge(self.sC.h, self.sC.n)
        gn = self.dmaG(self.mxA[0:1, :], self.dram["cc_mx_dst"].ap()[0:1, :])
        g.wait_ge(self.sG.h, gn)
        self.barrier()

    def prologue(self):
        nc = self.nc
        a = nc.alloc_sbuf_tensor
        self.ident_f = a("ident_f", [128, 128], F32).ap()
        self.ident_b = a("ident_b", [128, 128], BF16).ap()
        self.ones_f = a("ones_f", [128, 128], F32).ap()
        self.ones_b = a("ones_b", [128, 1], BF16).ap()
        self.consts = a("consts", [128, 8], F32).ap()
        self.rms2 = a("rms2", [128, 2], F32).ap()
        self.lamb = a("lamb", [128, H0], F32).ap()
        self.mxA = a("mxA", [1, 16], F32).ap()
        self.moe_gate_col = a("moegate", [128, 1], F32).ap()
        self.row = a("rowbuf", [1, 4096], F32).ap()
        self.smax_col = a("smax", [128, 1], F32).ap()
        eye_f = nc.inline_tensor(np.eye(128, dtype=np.float32), name="eye_f")
        eye_b = nc.inline_tensor(np.eye(128, dtype=np.float32).astype(BFnp), name="eye_b")
        self.dmaS(self.ident_f[:, :], eye_f.ap()[:, :])
        self.dmaS(self.ident_b[:, :], eye_b.ap()[:, :])
        self.dmaS(self.rms2[:, 0:1], self.in_["rms_scale"][0:128][:, None])
        self.dmaS(self.rms2[:, 1:2], self.in_["rms_scale"][128:256][:, None])
        self.dmaS(self.moe_gate_col[:, :], self.in_["moe_gate"][:, :])
        v, s, t = nc.vector, nc.scalar, nc.tensor
        v.memset(self.ones_f[:, :], 1.0)
        v.memset(self.ones_b[:, :], 1.0)
        v.memset(self.consts[:, 0:1], 1.0)
        v.memset(self.consts[:, 1:2], 1e-5)
        v.memset(self.consts[:, 2:3], RMS_EPS1)
        v.memset(self.consts[:, 3:4], 0.0)
        v.memset(self.consts[:, 4:5], 1e-12)
        v.memset(self.consts[:, 5:6], MAGIC)
        v.memset(self.consts[:, 6:7], -8.0)
        i = v.memset(self.mxA[0:1, :], 0.0)
        self.sV.inc(i)
        lt = a("lam_t", [8, 512], F32).ap()
        self.lr = lr = a("lam_r", [8, 8], F32).ap()
        self.dmaS(lt[0:8, 0:128], self.in_["lq1"][:, :])
        self.dmaS(lt[0:8, 128:256], self.in_["lk1"][:, :])
        self.dmaS(lt[0:8, 256:384], self.in_["lq2"][:, :])
        self.dmaS(lt[0:8, 384:512], self.in_["lk2"][:, :])
        v.wait_ge(self.sD.h, self.sD.n)
        v.tensor_mul(lt[0:8, 0:128], lt[0:8, 0:128], lt[0:8, 128:256])
        self.vs(v.tensor_mul(lt[0:8, 256:384], lt[0:8, 256:384], lt[0:8, 384:512]))
        v.tensor_reduce(lr[0:8, 0:1], lt[0:8, 0:128], axis=AX.X, op=ALU.add)
        i = v.tensor_reduce(lr[0:8, 1:2], lt[0:8, 256:384], axis=AX.X, op=ALU.add)
        self.sV.inc(i)
        s.wait_ge(self.sV.h, self.sV.n)
        i = s.activation(lr[0:8, 2:4], lr[0:8, 0:2], AF.Exp)
        self.sS.inc(i)
        v.wait_ge(self.sS.h, self.sS.n)
        self.vs(v.tensor_sub(lr[0:8, 0:1], lr[0:8, 2:3], lr[0:8, 3:4]))
        i = v.tensor_scalar(lr[0:8, 0:1], lr[0:8, 0:1], LAM_INIT, None, ALU.add)
        self.sV.inc(i)
        t.wait_ge(self.sV.h, self.sV.n)
        i = t.transpose(self.pb[7].ap()[0:1, 0:8], lr[0:8, 0:1], self.ident_f[0:8, 0:8])
        self.sT.inc(i)
        v.wait_ge(self.sT.h, self.sT.n)
        i = v.tensor_copy(self.row[0:1, 0:8], self.pb[7].ap()[0:1, 0:8])
        self.sV.inc(i)
        t.wait_ge(self.sV.h, self.sV.n)
        i = t.matmul(self.pb[7].ap()[:, 0:8], self.ones_f[0:1, 0:128], self.row[0:1, 0:8],
                     start=True, stop=True)
        self.sT.inc(i)
        v.wait_ge(self.sT.h, self.sT.n)
        i = v.tensor_copy(self.lamb[:, :], self.pb[7].ap()[:, 0:8])
        self.sV.inc(i)

    def emit_ln(self, src_tile_ap, lnT, col_base, xbuf, lnb, stat, absmax_col=None,
                lnf=None):
        nc = self.nc
        v, s, t = nc.vector, nc.scalar, nc.tensor
        xc, sqs = self.ln_xc, self.ln_sq
        self.nc.sync.wait_ge(self.sV.h, self.sV.n)
        dn = self.dmaS(xbuf, src_tile_ap)
        v.wait_ge(self.sD.h, dn)
        v.wait_ge(self.sS.h, self.sS.n)
        self.vs(v.tensor_reduce(stat[:, 0:1], xbuf, axis=AX.X, op=ALU.add))
        self.vs(v.tensor_scalar(stat[:, 1:2], stat[:, 0:1], -1.0 / D, None, ALU.mult))
        i = v.tensor_scalar(xc, xbuf, stat[:, 1:2], None, ALU.add)
        self.sV.inc(i)
        s.wait_ge(self.sV.h, self.sV.n)
        self.ss(s.activation(sqs, xc, AF.Square, accum_out=stat[:, 2:3]))
        i = s.activation(stat[:, 3:4], stat[:, 2:3], AF.Sqrt, bias=self.consts[:, 1:2],
                         scale=1.0 / D)
        self.sS.inc(i)
        v.wait_ge(self.sS.h, self.sS.n)
        self.vs(v.reciprocal(stat[:, 4:5], stat[:, 3:4]))
        tgt = lnf if lnf is not None else lnb
        v.wait_ge(self.sT.h, self.sT.n)
        self.vs(v.tensor_scalar(tgt, xc, stat[:, 4:5], None, ALU.mult))
        if lnf is not None:
            if absmax_col is not None:
                self.vs(v.tensor_reduce(stat[:, 5:6], lnf, axis=AX.X, op=ALU.max,
                                        apply_absolute_value=True))
                self.vs(v.tensor_max(absmax_col, absmax_col, stat[:, 5:6]))
            i = v.tensor_copy(lnb, lnf)
            self.sV.inc(i)
        ev = self.sV.n
        for dch in range(DCH):
            bank = self.pb[6 + (dch % 2)]
            pv = bank.ap()[:, 0:64].bitcast(BF16)
            t.wait_ge(self.sV.h, ev)
            i = t.transpose(pv[:, 0:128], lnb[:, dch * 128:(dch + 1) * 128], self.ident_b[:, :])
            self.sT.inc(i)
            v.wait_ge(self.sT.h, self.sT.n)
            i = v.tensor_copy(lnT[:, dch, col_base:col_base + 128], pv[:, 0:128])
            ev = self.sV.inc(i)

    def emit_ln_phase(self, src_tiles, lnT, absmax_col=None):
        nc = self.nc
        with nc.sbuf_tensor("ln_xb", [128, 2 * D], F32) as xb_h, \
             nc.sbuf_tensor("ln_lb", [128, 2 * D], BF16) as lb_h, \
             nc.sbuf_tensor("ln_lf", [128, D], F32) as lf_h, \
             nc.sbuf_tensor("ln_xc", [128, D], F32) as xc_h, \
             nc.sbuf_tensor("ln_sq", [128, D], F32) as sq_h2, \
             nc.sbuf_tensor("ln_st", [128, 2 * 16], F32) as st_h:
            xb, lb, st = xb_h.ap(), lb_h.ap(), st_h.ap()
            lf = lf_h.ap()
            self.ln_xc, self.ln_sq = xc_h.ap(), sq_h2.ap()
            for ti in range(NT):
                p = ti % 2
                self.emit_ln(src_tiles[ti, :, :], lnT, ti * 128,
                             xb[:, p * D:(p + 1) * D], lb[:, p * D:(p + 1) * D],
                             st[:, p * 16:(p + 1) * 16],
                             absmax_col=absmax_col,
                             lnf=(lf[:, :] if absmax_col is not None else None))

    def emit_allgather_ln(self, lnT, src_name, dst_name):
        nc = self.nc
        g = nc.gpsimd
        self.barrier()
        gn = self.dmaG(self.dram[src_name].ap()[:, :], lnT[:, :, 0:TOK])
        g.wait_ge(self.sG.h, gn)
        i = g.collective_compute("AllGather", ALU.bypass, replica_groups=[list(range(NCORES))],
                                 ins=[self.dram[src_name].ap().opt()],
                                 outs=[self.dram[dst_name].ap().opt()])
        self.sC.inc(i)
        g.wait_ge(self.sC.h, self.sC.n)
        pid = nc.partition_id(engines=[mybir.EngineType.Pool])
        peer = self.nc.s_assert_within(pid + 1 - 2 * (pid % 2), min_val=0,
                                       max_val=NCORES - 1, skip_runtime_assert=True)
        dst_view = self.dram[dst_name].ap().rearrange("r p (c k) -> r p c k", c=DCH)
        gn = self.dmaG(lnT[:, :, TOK:NSEQ], dst_view[bass.ds(peer, 1), :, :, :])
        g.wait_ge(self.sG.h, gn)
        self.barrier()

    def l0_attn(self):
        nc = self.nc
        v, s, t, sy = nc.vector, nc.scalar, nc.tensor, nc.sync
        x_tiles = self.in_["x_own"].rearrange("(n p) d -> n p d", p=128)
        with nc.sbuf_tensor("masks_sb", [128, 8 * 4 * 256], BF16) as masks_h, \
             nc.sbuf_tensor("lnT0", [128, DCH, NSEQ], BF16) as lnT_h:
            masks, lnT = masks_h.ap(), lnT_h.ap()
            self.dmaS(masks[:, :], self.in_["masks"][:, :])
            self.emit_ln_phase(x_tiles, lnT)
            self.tap("lnT_own", lnT[:, :, 0:TOK])
            self.emit_allgather_ln(lnT, "cc_ln0_src", "cc_ln0_dst")
            self.tap("lnT_all", lnT[:, :, :])
            with nc.sbuf_tensor("wh", [128, 2, DCH, 768], BF16) as wh_h, \
                 nc.sbuf_tensor("kT", [128, 2, NSEQ], BF16) as kT_h, \
                 nc.sbuf_tensor("qT", [128, 2, TOK], BF16) as qT_h, \
                 nc.sbuf_tensor("vA", [128, 32, 257], BF16) as vA_h, \
                 nc.sbuf_tensor("Pb", [128, 4, 256], BF16) as Pb_h, \
                 nc.sbuf_tensor("hsc", [128, 16], F32) as hsc_h, \
                 nc.sbuf_tensor("sqc", [128, 512], BF16) as sq_h, \
                 nc.sbuf_tensor("nrm", [1, NSEQ], F32) as nrm_h, \
                 nc.sbuf_tensor("gbuf", [128, 768], F32) as gb_h, \
                 nc.sbuf_tensor("gb16", [128, 256], BF16) as gb16_h:
                ctx = dict(wh=wh_h.ap(), kT=kT_h.ap(), qT=qT_h.ap(), vA=vA_h.ap(),
                           Pb=Pb_h.ap(), hsc=hsc_h.ap(), sqc=sq_h.ap(), nrm=nrm_h.ap(),
                           gbuf=gb_h.ap(), gb16=gb16_h.ap(), masks=masks, lnT=lnT)
                wsnap = {0: self.load_l0_head_w(0, ctx)}
                self.proj_done = {}
                for h in range(H0):
                    if h + 1 < H0:
                        if h >= 1:
                            nc.gpsimd.wait_ge(self.sT.h, self.proj_done[h - 1])
                        wsnap[h + 1] = self.load_l0_head_w(h + 1, ctx)
                    self.l0_head(h, ctx, wsnap[h])
        self.barrier()
        with nc.sbuf_tensor("wof", [128, 16, D], F32) as wof_h, \
             nc.sbuf_tensor("wob", [128, 16, D], BF16) as wob_h, \
             nc.sbuf_tensor("gts", [128, 2, 16, 128], BF16) as gts_h, \
             nc.sbuf_tensor("rt", [128, 2 * D], F32) as rt_h:
            wof, wob, gts, rt = wof_h.ap(), wob_h.ap(), gts_h.ap(), rt_h.ap()
            wsrc = self.in_["wo"].rearrange("(c p) d -> p c d", p=128)
            dn = self.dmaS(wof[:, :, :], wsrc[:, :, :])
            v.wait_ge(self.sD.h, dn)
            for ch in range(16):
                i = v.tensor_scalar(wob[:, ch, :], wof[:, ch, :],
                                    self.rms2[:, (ch % 2):(ch % 2) + 1], 0.2,
                                    ALU.mult, ALU.mult)
            self.sV.inc(i)
            gt_tiles = self.dram["gt_hbm"].ap()
            evac_snap = [0, 0]
            store_snap = [0, 0]
            for ti in range(NT):
                pg = ti % 2
                sy.wait_ge(self.sD.h, store_snap[pg])
                dn = self.dmaS(gts[:, pg, :, :],
                               gt_tiles[:, :, ti * 128:(ti + 1) * 128]
                               .rearrange("c p k -> p c k"))
                b0 = self.pb[pg * 2]
                b1 = self.pb[pg * 2 + 1]
                t.wait_ge(self.sD.h, dn)
                t.wait_ge(self.sV.h, self.sV.n if ti < 2 else evac_snap[pg])
                for fc in range(16):
                    st0 = (fc == 0)
                    sp = (fc == 15)
                    t.matmul(b0.ap()[:, 0:512], gts[:, pg, fc, :], wob[:, fc, 0:512],
                             start=st0, stop=sp)
                    i = t.matmul(b1.ap()[:, 0:512], gts[:, pg, fc, :], wob[:, fc, 512:1024],
                                 start=st0, stop=sp)
                self.sT.inc(i)
                tsnap = self.sT.n
                dn = self.dmaS(rt[:, pg * D:pg * D + D], x_tiles[ti, :, :])
                v.wait_ge(self.sD.h, dn)
                v.wait_ge(self.sT.h, tsnap)
                v.tensor_add(rt[:, pg * D:pg * D + 512], b0.ap()[:, 0:512],
                             rt[:, pg * D:pg * D + 512])
                i = v.tensor_add(rt[:, pg * D + 512:pg * D + 1024], b1.ap()[:, 0:512],
                                 rt[:, pg * D + 512:pg * D + 1024])
                evac_snap[pg] = self.sV.inc(i)
                sy.wait_ge(self.sV.h, evac_snap[pg])
                store_snap[pg] = self.dmaS(self.dram["resid_hbm"].ap()[ti, :, :],
                                           rt[:, pg * D:pg * D + D])
            sy.wait_ge(self.sD.h, self.sD.n)
        self.barrier()
        self.tap("resid1", self.dram["resid_hbm"].ap()[:, :, :])
        self.tap("gt", self.dram["gt_hbm"].ap()[:, :, :])

    def load_l0_head_w(self, h, ctx):
        wh = ctx["wh"]
        sl = wh[:, h % 2]
        n = 0
        for j, nm in enumerate(("wq", "wk", "wv")):
            src = self.in_[nm].rearrange("(c p) f -> p c f", p=128)
            n = self.dmaG(sl[:, :, j * 256:(j + 1) * 256], src[:, :, h * 256:(h + 1) * 256])
        return n

    def l0_head(self, h, ctx, wsnap):
        nc = self.nc
        v, s, t = nc.vector, nc.scalar, nc.tensor
        wh, kT, qT, vA = ctx["wh"], ctx["kT"], ctx["qT"], ctx["vA"]
        Pb, hsc, sqc, nrm = ctx["Pb"], ctx["hsc"], ctx["sqc"], ctx["nrm"]
        gbuf, gb16, masks, lnT = ctx["gbuf"], ctx["gb16"], ctx["masks"], ctx["lnT"]
        whh = wh[:, h % 2]
        t.wait_ge(self.sG.h, wsnap)
        bank_ev = {}

        def proj(bank_i, lhsT, rhs, ncol, dst):
            bank = self.pb[bank_i]
            if bank_i in bank_ev:
                t.wait_ge(self.sV.h, bank_ev[bank_i])
            for dch in range(DCH):
                i = t.matmul(bank.ap()[:, 0:ncol], lhsT(dch), rhs(dch),
                             start=(dch == 0), stop=(dch == DCH - 1))
            self.sT.inc(i)
            v.wait_ge(self.sT.h, self.sT.n)
            i = v.tensor_copy(dst, bank.ap()[:, 0:ncol])
            bank_ev[bank_i] = self.sV.inc(i)

        for st_i in range(2):
            for ck in range(8):
                proj(ck % 2,
                     lambda dch, st_i=st_i: whh[:, dch, 256 + st_i * 128:256 + st_i * 128 + 128],
                     lambda dch, ck=ck: lnT[:, dch, ck * 512:(ck + 1) * 512],
                     512, kT[:, st_i, ck * 512:(ck + 1) * 512])
            for cq in range(4):
                proj(2 + cq % 2,
                     lambda dch, st_i=st_i: whh[:, dch, st_i * 128:st_i * 128 + 128],
                     lambda dch, cq=cq: lnT[:, dch, cq * 512:(cq + 1) * 512],
                     512, qT[:, st_i, cq * 512:(cq + 1) * 512])
        for kt in range(32):
            proj(4 + kt % 2,
                 lambda dch, kt=kt: lnT[:, dch, kt * 128:(kt + 1) * 128],
                 lambda dch: whh[:, dch, 512:768],
                 256, vA[:, kt, 0:256])
        self.proj_done[h] = self.sT.n
        i = v.memset(vA[:, :, 256:257], 1.0)
        self.sV.inc(i)
        for st_i in range(2):
            for which, src, ncols, dst_i in ((0, kT, 8, 8 + st_i), (1, qT, 4, 10 + st_i)):
                for ck in range(ncols):
                    i = v.tensor_mul(sqc[:, 0:512], src[:, st_i, ck * 512:(ck + 1) * 512],
                                     src[:, st_i, ck * 512:(ck + 1) * 512])
                    vs = self.sV.inc(i)
                    t.wait_ge(self.sV.h, vs)
                    i = t.matmul(self.pb[6].ap()[0:1, 0:512], self.ones_b[:, 0:1],
                                 sqc[:, 0:512], start=True, stop=True)
                    ts_ = self.sT.inc(i)
                    v.wait_ge(self.sT.h, ts_)
                    i = v.tensor_copy(nrm[0:1, ck * 512:(ck + 1) * 512],
                                      self.pb[6].ap()[0:1, 0:512])
                    self.vs(i)
                self.vs(v.tensor_reduce(hsc[0:1, dst_i:dst_i + 1], nrm[0:1, 0:ncols * 512],
                                        axis=AX.X, op=ALU.max))
            i = v.tensor_mul(hsc[0:1, 12 + st_i:13 + st_i], hsc[0:1, 8 + st_i:9 + st_i],
                             hsc[0:1, 10 + st_i:11 + st_i])
            self.sV.inc(i)
            s.wait_ge(self.sV.h, self.sV.n)
            i = s.activation(hsc[0:1, 14 + st_i:15 + st_i], hsc[0:1, 12 + st_i:13 + st_i],
                             AF.Sqrt, bias=self.consts[0:1, 3:4], scale=1.0)
            self.sS.inc(i)
            v.wait_ge(self.sS.h, self.sS.n)
            self.vs(v.tensor_scalar(hsc[0:1, 14 + st_i:15 + st_i],
                                    hsc[0:1, 14 + st_i:15 + st_i],
                                    -SC0 * 1.00001, None, ALU.mult))
            self.bcast128(hsc[:, st_i:st_i + 1], hsc[0:1, 14 + st_i:15 + st_i])
        if h == 0:
            self.tap("h0_kT", kT[:, :, :])
            self.tap("h0_qT", qT[:, :, :])
            self.tap("h0_vA", vA[:, :, :])
            self.tap("h0_hsc", hsc[:, :])
        s.wait_ge(self.sV.h, self.sV.n)
        for qc in range(8):
            nkt = 2 * qc + 2
            units = [(rng, kt) for rng in range(2) for kt in range(nkt)]
            nu = len(units)
            exp_snap = {}
            ready = {}
            for u in range(nu + 1):
                if u < nu:
                    rng, kt = units[u]
                    sb = self.pb[u % 2]
                    if u >= 2:
                        t.wait_ge(self.sS.h, exp_snap[u - 2])
                    kbase = rng * TOK + kt * 128
                    t.matmul(sb.ap()[:, 0:256], kT[:, 0, kbase:kbase + 128],
                             qT[:, 0, qc * 256:(qc + 1) * 256], start=True, stop=True)
                    i = t.matmul(sb.ap()[:, 256:512], kT[:, 1, kbase:kbase + 128],
                                 qT[:, 1, qc * 256:(qc + 1) * 256], start=True, stop=True)
                    tsnap = self.sT.inc(i)
                    s.wait_ge(self.sT.h, tsnap)
                    s.activation(Pb[:, (u % 2) * 2, :], sb.ap()[:, 0:256], AF.Exp,
                                 bias=hsc[:, 0:1], scale=SC0)
                    i = s.activation(Pb[:, (u % 2) * 2 + 1, :], sb.ap()[:, 256:512], AF.Exp,
                                     bias=hsc[:, 1:2], scale=SC0)
                    exp_snap[u] = self.sS.inc(i)
                    if kt >= 2 * qc:
                        mi = qc * 4 + rng * 2 + (kt - 2 * qc)
                        v.wait_ge(self.sS.h, exp_snap[u])
                        v.tensor_mul(Pb[:, (u % 2) * 2, :], Pb[:, (u % 2) * 2, :],
                                     masks[:, mi * 256:(mi + 1) * 256])
                        i = v.tensor_mul(Pb[:, (u % 2) * 2 + 1, :], Pb[:, (u % 2) * 2 + 1, :],
                                         masks[:, mi * 256:(mi + 1) * 256])
                        ready[u] = ("V", self.sV.inc(i))
                    else:
                        ready[u] = ("S", exp_snap[u])
                if u > 0:
                    rng0, kt0 = units[u - 1]
                    kind, val = ready[u - 1]
                    t.wait_ge(self.sS.h if kind == "S" else self.sV.h, val)
                    kti = rng0 * 16 + kt0
                    st0 = (u - 1 == 0)
                    sp = (u - 1 == nu - 1)
                    for st_i in range(2):
                        for sub in range(2):
                            i = t.matmul(self.pb[2 + st_i * 2 + sub].ap()[:, 0:257],
                                         Pb[:, ((u - 1) % 2) * 2 + st_i,
                                            sub * 128:(sub + 1) * 128],
                                         vA[:, kti, 0:257], start=st0, stop=sp)
                    self.sT.inc(i)
            v.wait_ge(self.sT.h, self.sT.n)
            for sub in range(2):
                o1 = self.pb[2 + sub].ap()
                o2 = self.pb[4 + sub].ap()
                v.reciprocal(hsc[:, 4:5], o1[:, 256:257])
                self.vs(v.reciprocal(hsc[:, 5:6], o2[:, 256:257]))
                v.tensor_scalar(hsc[:, 6:7], hsc[:, 5:6], self.lamb[:, h:h + 1], -1.0,
                                ALU.mult, ALU.mult)
                self.vs(v.tensor_scalar(gbuf[:, 0:256], o1[:, 0:256], hsc[:, 4:5], None,
                                        ALU.mult))
                self.vs(v.scalar_tensor_tensor(gbuf[:, 256:512], o2[:, 0:256], hsc[:, 6:7],
                                               gbuf[:, 0:256], ALU.mult, ALU.add))
                self.vs(v.tensor_mul(gbuf[:, 512:768], gbuf[:, 256:512], gbuf[:, 256:512]))
                i = v.tensor_reduce(hsc[:, 7:8], gbuf[:, 512:768], axis=AX.X, op=ALU.add)
                self.sV.inc(i)
                s.wait_ge(self.sV.h, self.sV.n)
                i = s.activation(hsc[:, 4:5], hsc[:, 7:8], AF.Sqrt, bias=self.consts[:, 1:2],
                                 scale=1.0 / 256.0)
                self.sS.inc(i)
                v.wait_ge(self.sS.h, self.sS.n)
                self.vs(v.reciprocal(hsc[:, 5:6], hsc[:, 4:5]))
                v.wait_ge(self.sT.h, self.sT.n)
                i = v.tensor_scalar(gb16[:, 0:256], gbuf[:, 256:512], hsc[:, 5:6], None,
                                    ALU.mult)
                vs = self.sV.inc(i)
                t.wait_ge(self.sV.h, vs)
                v.wait_ge(self.sD.h, self.sD.n)
                for fc in range(2):
                    bank = self.pb[6 + fc]
                    pv = bank.ap()[:, 0:64].bitcast(BF16)
                    i = t.transpose(pv[:, 0:128], gb16[:, fc * 128:(fc + 1) * 128],
                                    self.ident_b[:, :])
                    ts_ = self.sT.inc(i)
                    v.wait_ge(self.sT.h, ts_)
                    i = v.tensor_copy(self.gtev[:, fc, :], pv[:, 0:128])
                    vs2 = self.sV.inc(i)
                self.nc.sync.wait_ge(self.sV.h, vs2)
                self.dmaS(self.dram["gt_hbm"].ap()[2 * h:2 * h + 2, :,
                                                   qc * 256 + sub * 128:
                                                   qc * 256 + sub * 128 + 128]
                          .rearrange("c p k -> p c k"),
                          self.gtev[:, :, :])
            t.wait_ge(self.sV.h, self.sV.n)

    def qdq(self, out_ap, in_ap, rcol, scol, ncol):
        v = self.nc.vector
        mg = self.consts[:, 5:6].broadcast_to([128, ncol])
        self.vs(v.scalar_tensor_tensor(self.qtmp[:, 0:ncol], in_ap, rcol, mg,
                                       ALU.mult, ALU.add))
        self.vs(v.tensor_scalar(out_ap, self.qtmp[:, 0:ncol], -MAGIC, scol,
                                ALU.add, ALU.mult))

    def mk_scale(self, idx, s_col, r_col, pre=1.0 / QMAX, rmul=1.0):
        nc = self.nc
        v = nc.vector
        v.wait_ge(self.sG.h, self.sG.n)
        self.vs(v.tensor_scalar(self.mxA[0:1, 8:9], self.mxA[0:1, idx:idx + 1],
                                pre, 1e-12, ALU.mult, ALU.max))
        self.vs(v.reciprocal(self.mxA[0:1, 9:10], self.mxA[0:1, 8:9]))
        if rmul != 1.0:
            self.vs(v.tensor_scalar(self.mxA[0:1, 9:10], self.mxA[0:1, 9:10],
                                    rmul, None, ALU.mult))
        self.bcast128(s_col, self.mxA[0:1, 8:9])
        self.bcast128(r_col, self.mxA[0:1, 9:10])

    def dequant_tile(self, wq_bf, wnat, dq, nblk):
        v = self.nc.vector
        ncol = nblk * 32
        wv = wnat.rearrange("p (g k) -> p g k", k=32)
        self.vs(v.tensor_reduce(dq[:, 0:nblk], wv, axis=AX.X, op=ALU.max,
                                apply_absolute_value=True))
        self.vs(v.tensor_scalar(dq[:, 0:nblk], dq[:, 0:nblk], 1.0 / 7.0, 1e-12,
                                ALU.mult, ALU.max))
        self.vs(v.reciprocal(dq[:, nblk:2 * nblk], dq[:, 0:nblk]))
        rb = dq[:, nblk:2 * nblk].rearrange("p (g o) -> p g o", o=1).broadcast_to(
            [128, nblk, 32])
        sb = dq[:, 0:nblk].rearrange("p (g o) -> p g o", o=1).broadcast_to(
            [128, nblk, 32])
        qv = self.qtmp[:, 0:ncol].rearrange("p (g k) -> p g k", k=32)
        self.vs(v.tensor_tensor(qv, wv, rb, ALU.mult))
        self.vs(v.tensor_scalar(self.qtmp[:, 0:ncol], self.qtmp[:, 0:ncol], MAGIC, -MAGIC,
                                ALU.add, ALU.add))
        self.vs(v.tensor_tensor(wq_bf.rearrange("p (g k) -> p g k", k=32), qv, sb,
                                ALU.mult))

    def l0_ffn(self):
        nc = self.nc
        v, s, t, sy, g = nc.vector, nc.scalar, nc.tensor, nc.sync, nc.gpsimd
        rt_tiles = self.dram["resid_hbm"].ap()
        with nc.sbuf_tensor("lnT2", [128, DCH, TOK], BF16) as lnT2_h, \
             nc.sbuf_tensor("qtmp", [128, 3072], F32) as qtmp_h, \
             nc.sbuf_tensor("scol", [128, 12], F32) as scol_h:
            lnT2 = lnT2_h.ap()
            self.qtmp = qtmp_h.ap()
            scol = scol_h.ap()
            i = v.memset(self.mxA[0:1, 0:8], 0.0)
            self.vs(i)
            i = v.memset(self.smax_col, 0.0)
            self.vs(i)
            with nc.sbuf_tensor("ln2f", [128, NT, D], F32) as ln2f_h, \
                 nc.sbuf_tensor("ln2_xb", [128, 2 * D], F32) as xb_h, \
                 nc.sbuf_tensor("ln2_xc", [128, D], F32) as xc_h, \
                 nc.sbuf_tensor("ln2_sq", [128, D], F32) as sq_h2, \
                 nc.sbuf_tensor("ln2_st", [128, 2 * 16], F32) as st_h:
                ln2f, xb, st = ln2f_h.ap(), xb_h.ap(), st_h.ap()
                self.ln_xc, self.ln_sq = xc_h.ap(), sq_h2.ap()
                for ti in range(NT):
                    p = ti % 2
                    self.emit_ln_f32(rt_tiles[ti, :, :], ln2f[:, ti, :],
                                     xb[:, p * D:(p + 1) * D], st[:, p * 16:(p + 1) * 16],
                                     self.smax_col)
                self.part_max(self.mxA[0:1, 0:1], self.smax_col)
                self.allreduce_max(0)
                self.mk_scale(0, scol[:, 0:1], scol[:, 1:2])
                with nc.sbuf_tensor("lnb2", [128, D], BF16) as lnb2_h:
                    lnb2 = lnb2_h.ap()
                    for ti in range(NT):
                        v.wait_ge(self.sT.h, self.sT.n)
                        self.qdq(lnb2[:, :], ln2f[:, ti, :], scol[:, 1:2], scol[:, 0:1], D)
                        ev = self.sV.n
                        for dch in range(DCH):
                            bank = self.pb[6 + (dch % 2)]
                            pv = bank.ap()[:, 0:64].bitcast(BF16)
                            t.wait_ge(self.sV.h, ev)
                            i = t.transpose(pv[:, 0:128], lnb2[:, dch * 128:(dch + 1) * 128],
                                            self.ident_b[:, :])
                            self.sT.inc(i)
                            v.wait_ge(self.sT.h, self.sT.n)
                            i = v.tensor_copy(lnT2[:, dch, ti * 128:(ti + 1) * 128],
                                              pv[:, 0:128])
                            ev = self.sV.inc(i)
            with nc.sbuf_tensor("wupT", [128, DCH, INTER], BF16) as wupT_h, \
                 nc.sbuf_tensor("wgT", [128, DCH, INTER], BF16) as wgT_h, \
                 nc.sbuf_tensor("wnat", [128, 2 * D], F32) as wnat_h, \
                 nc.sbuf_tensor("wdq", [128, 64], F32) as wdq_h, \
                 nc.sbuf_tensor("wqb", [128, D], BF16) as wqb_h:
                wupT, wgT = wupT_h.ap(), wgT_h.ap()
                wnat, wdq, wqb = wnat_h.ap(), wdq_h.ap(), wqb_h.ap()
                for wt, nm in ((wupT, "wup"), (wgT, "wgate")):
                    src = self.in_[nm].rearrange("(c p) d -> c p d", p=128)
                    ld = {}
                    for rt_i in range(ICH):
                        p = rt_i % 2
                        sy.wait_ge(self.sV.h, self.sV.n)
                        ld[rt_i] = self.dmaS(wnat[:, p * D:(p + 1) * D], src[rt_i, :, :])
                        v.wait_ge(self.sD.h, ld[rt_i])
                        v.wait_ge(self.sT.h, self.sT.n)
                        self.dequant_tile(wqb[:, :], wnat[:, p * D:(p + 1) * D], wdq, 32)
                        ev = self.sV.n
                        for dch in range(DCH):
                            bank = self.pb[6 + (dch % 2)]
                            pv = bank.ap()[:, 0:64].bitcast(BF16)
                            t.wait_ge(self.sV.h, ev)
                            i = t.transpose(pv[:, 0:128], wqb[:, dch * 128:(dch + 1) * 128],
                                            self.ident_b[:, :])
                            self.sT.inc(i)
                            v.wait_ge(self.sT.h, self.sT.n)
                            i = v.tensor_copy(wt[:, dch, rt_i * 128:(rt_i + 1) * 128],
                                              pv[:, 0:128])
                            ev = self.sV.inc(i)
                self.barrier()
                with nc.sbuf_tensor("ubuf", [128, 2 * 512], F32) as ubuf_h, \
                     nc.sbuf_tensor("mx2", [128, 4], F32) as mx2_h:
                    ubuf, mx2 = ubuf_h.ap(), mx2_h.ap()
                    i = v.memset(mx2[:, :], 0.0)
                    self.vs(i)
                    for wi, (wt, spname, mcol) in enumerate(
                            ((wupT, "sp_up", 0), (wgT, "sp_gate", 1))):
                        sp = self.dram[spname].ap()
                        st_snap = {}
                        for ic in range(ICH):
                            for tc in range(4):
                                u = ic * 4 + tc
                                bank = self.pb[u % 2]
                                for dch in range(DCH):
                                    i = t.matmul(bank.ap()[:, 0:512],
                                                 wt[:, dch, ic * 128:(ic + 1) * 128],
                                                 lnT2[:, dch, tc * 512:(tc + 1) * 512],
                                                 start=(dch == 0), stop=(dch == DCH - 1))
                                self.sT.inc(i)
                                pbuf = u % 2
                                v.wait_ge(self.sT.h, self.sT.n)
                                v.wait_ge(self.sD.h, st_snap.get(pbuf, 0))
                                i = v.tensor_copy(ubuf[:, pbuf * 512:(pbuf + 1) * 512],
                                                  bank.ap()[:, 0:512])
                                vs = self.vs(i)
                                self.vs(v.tensor_reduce(
                                    mx2[:, 2:3], ubuf[:, pbuf * 512:(pbuf + 1) * 512],
                                    axis=AX.X, op=ALU.max, apply_absolute_value=True))
                                self.vs(v.tensor_max(mx2[:, mcol:mcol + 1],
                                                     mx2[:, mcol:mcol + 1], mx2[:, 2:3]))
                                sy.wait_ge(self.sV.h, vs)
                                st_snap[pbuf] = self.dmaS(
                                    sp[ic, :, tc * 512:(tc + 1) * 512],
                                    ubuf[:, pbuf * 512:(pbuf + 1) * 512])
                                t.wait_ge(self.sV.h, vs)
                    self.part_max(self.mxA[0:1, 1:2], mx2[:, 0:1])
                    self.part_max(self.mxA[0:1, 2:3], mx2[:, 1:2])
                self.allreduce_max(1)
            self.mk_scale(1, scol[:, 2:3], scol[:, 3:4])
            self.mk_scale(2, scol[:, 4:5], scol[:, 5:6])
            with nc.sbuf_tensor("cbuf", [128, 6 * 512], F32) as cbuf_h, \
                 nc.sbuf_tensor("mx3", [128, 4], F32) as mx2_h:
                cbuf, mx2 = cbuf_h.ap(), mx2_h.ap()
                i = v.memset(mx2[:, :], 0.0)
                self.vs(i)
                spg = self.dram["sp_gate"].ap()
                spt = self.dram["sp_t"].ap()
                st_snap = {}
                for u in range(ICH * 4):
                    ic, tc = divmod(u, 4)
                    p = u % 2
                    gb = cbuf[:, (p * 3) * 512:(p * 3 + 1) * 512]
                    sgb = cbuf[:, (p * 3 + 1) * 512:(p * 3 + 2) * 512]
                    tb = cbuf[:, (p * 3 + 2) * 512:(p * 3 + 3) * 512]
                    sy.wait_ge(self.sD.h, st_snap.get(p, 0))
                    sy.wait_ge(self.sV.h, self.sV.n)
                    dn = self.dmaS(gb, spg[ic, :, tc * 512:(tc + 1) * 512])
                    v.wait_ge(self.sD.h, dn)
                    self.qdq(gb, gb, scol[:, 5:6], scol[:, 4:5], 512)
                    s.wait_ge(self.sV.h, self.sV.n)
                    i = s.activation(sgb, gb, AF.Sigmoid)
                    self.sS.inc(i)
                    v.wait_ge(self.sS.h, self.sS.n)
                    self.vs(v.tensor_scalar(sgb, sgb, 65536.0, MAGIC, ALU.mult, ALU.add))
                    self.vs(v.tensor_scalar(sgb, sgb, -MAGIC, 65535.0, ALU.add, ALU.min))
                    self.vs(v.tensor_tensor(tb, gb, sgb, ALU.mult))
                    self.vs(v.tensor_reduce(mx2[:, 2:3], tb, axis=AX.X, op=ALU.max,
                                            apply_absolute_value=True))
                    self.vs(v.tensor_max(mx2[:, 0:1], mx2[:, 0:1], mx2[:, 2:3]))
                    sy.wait_ge(self.sV.h, self.sV.n)
                    st_snap[p] = self.dmaS(spt[ic, :, tc * 512:(tc + 1) * 512], tb)
                    if u == 0:
                        self.tap("f3_gb", gb)
                        self.tap("f3_sgb", sgb)
                        self.tap("f3_mx", mx2[:, :])
                self.tap("f3_mx_end", mx2[:, :])
                self.part_max(self.mxA[0:1, 3:4], mx2[:, 0:1])
                self.vs(v.tensor_scalar(self.mxA[0:1, 3:4], self.mxA[0:1, 3:4],
                                        1.0 / 65536.0, None, ALU.mult))
                self.tap("f3_premax", self.mxA[0:1, :])
                self.allreduce_max(3)
                self.tap("f3_postmax", self.mxA[0:1, :])
                self.mk_scale(3, scol[:, 6:7], scol[:, 7:8], rmul=1.0 / 65536.0)
                i = v.memset(mx2[:, 0:2], 0.0)
                self.vs(i)
                spu = self.dram["sp_up"].ap()
                st_snap = {}
                for u in range(ICH * 4):
                    ic, tc = divmod(u, 4)
                    p = u % 2
                    g2 = cbuf[:, (p * 3) * 512:(p * 3 + 1) * 512]
                    ub = cbuf[:, (p * 3 + 1) * 512:(p * 3 + 2) * 512]
                    pr = cbuf[:, (p * 3 + 2) * 512:(p * 3 + 3) * 512]
                    sy.wait_ge(self.sD.h, st_snap.get(p, 0))
                    sy.wait_ge(self.sV.h, self.sV.n)
                    dn = self.dmaS(g2, spt[ic, :, tc * 512:(tc + 1) * 512])
                    dn = self.dmaS(ub, spu[ic, :, tc * 512:(tc + 1) * 512])
                    v.wait_ge(self.sD.h, dn)
                    if u == 0:
                        self.tap("f4_t_in", g2)
                        self.tap("f4_up_in", ub)
                    self.qdq(g2, g2, scol[:, 7:8], scol[:, 6:7], 512)
                    self.qdq(ub, ub, scol[:, 3:4], scol[:, 2:3], 512)
                    self.vs(v.tensor_tensor(pr, g2, ub, ALU.mult))
                    if u == 0:
                        self.tap("f4_g2", g2)
                        self.tap("f4_ub", ub)
                        self.tap("f4_scol", scol[:, :])
                    self.vs(v.tensor_reduce(mx2[:, 2:3], pr, axis=AX.X, op=ALU.max,
                                            apply_absolute_value=True))
                    self.vs(v.tensor_max(mx2[:, 0:1], mx2[:, 0:1], mx2[:, 2:3]))
                    sy.wait_ge(self.sV.h, self.sV.n)
                    st_snap[p] = self.dmaS(spg[ic, :, tc * 512:(tc + 1) * 512], pr)
                self.part_max(self.mxA[0:1, 4:5], mx2[:, 0:1])
                self.allreduce_max(4)
                self.mk_scale(4, scol[:, 8:9], scol[:, 9:10])
            self.tap("scol", scol[:, :])
            self.tap("mxa", self.mxA[0:1, :])
            self.tap("lnT2", lnT2[:, :, :])
            self.tap("spup", self.dram["sp_up"].ap()[0, :, :])
            self.tap("spt", self.dram["sp_t"].ap()[0, :, :])
            self.tap("spu2", self.dram["sp_gate"].ap()[0, :, :])
            with nc.sbuf_tensor("wdT", [128, ICH, D], BF16) as wdT_h, \
                 nc.sbuf_tensor("wnat_d", [128, 3072], F32) as wnat_h, \
                 nc.sbuf_tensor("wdq_d", [128, 192], F32) as wdq_h, \
                 nc.sbuf_tensor("wqb_d", [128, 3072], BF16) as wqb_h:
                wdT, wnat, wdq, wqb = wdT_h.ap(), wnat_h.ap(), wdq_h.ap(), wqb_h.ap()
                src = self.in_["wdown"].rearrange("(c p) i -> c p i", p=128)
                for rt_i in range(DCH):
                    sy.wait_ge(self.sV.h, self.sV.n)
                    dn = self.dmaS(wnat[:, :], src[rt_i, :, :])
                    v.wait_ge(self.sD.h, dn)
                    v.wait_ge(self.sT.h, self.sT.n)
                    self.dequant_tile(wqb[:, :], wnat[:, :], wdq, 96)
                    ev = self.sV.n
                    for ich in range(ICH):
                        bank = self.pb[6 + (ich % 2)]
                        pv = bank.ap()[:, 0:64].bitcast(BF16)
                        t.wait_ge(self.sV.h, ev)
                        i = t.transpose(pv[:, 0:128], wqb[:, ich * 128:(ich + 1) * 128],
                                        self.ident_b[:, :])
                        self.sT.inc(i)
                        v.wait_ge(self.sT.h, self.sT.n)
                        i = v.tensor_copy(wdT[:, ich, rt_i * 128:(rt_i + 1) * 128],
                                          pv[:, 0:128])
                        ev = self.sV.inc(i)
                self.barrier()
                with nc.sbuf_tensor("obuf", [128, 2 * 512], BF16) as obuf_h, \
                     nc.sbuf_tensor("ubuf2", [128, 2 * 512], F32) as ubuf2_h, \
                     nc.sbuf_tensor("rt2", [128, 4 * D], F32) as rt2_h:
                    obuf, ubuf2, rt2 = obuf_h.ap(), ubuf2_h.ap(), rt2_h.ap()
                    spg = self.dram["sp_gate"].ap()
                    for tg in range(4):
                        ld_snap = {}
                        for ic in range(ICH):
                            p = ic % 2
                            ub = ubuf2[:, p * 512:(p + 1) * 512]
                            ob = obuf[:, p * 512:(p + 1) * 512]
                            sy.wait_ge(self.sV.h, self.sV.n)
                            sy.wait_ge(self.sT.h, ld_snap.get(p, 0))
                            dn = self.dmaS(ub, spg[ic, :, tg * 512:(tg + 1) * 512])
                            v.wait_ge(self.sD.h, dn)
                            v.wait_ge(self.sT.h, ld_snap.get(p, 0))
                            self.qdq(ob, ub, scol[:, 9:10], scol[:, 8:9], 512)
                            t.wait_ge(self.sV.h, self.sV.n)
                            for sub in range(4):
                                for half in range(2):
                                    i = t.matmul(
                                        self.pb[sub * 2 + half].ap()[:, 0:512],
                                        ob[:, sub * 128:(sub + 1) * 128],
                                        wdT[:, ic, half * 512:(half + 1) * 512],
                                        start=(ic == 0), stop=(ic == ICH - 1))
                            ld_snap[p] = self.sT.inc(i)
                        tdone = self.sT.n
                        for sub in range(4):
                            ti = tg * 4 + sub
                            dn = self.dmaS(rt2[:, sub * D:(sub + 1) * D], rt_tiles[ti, :, :])
                            v.wait_ge(self.sD.h, dn)
                            v.wait_ge(self.sT.h, tdone)
                            v.tensor_add(rt2[:, sub * D:sub * D + 512],
                                         self.pb[sub * 2].ap()[:, 0:512],
                                         rt2[:, sub * D:sub * D + 512])
                            i = v.tensor_add(rt2[:, sub * D + 512:sub * D + 1024],
                                             self.pb[sub * 2 + 1].ap()[:, 0:512],
                                             rt2[:, sub * D + 512:sub * D + 1024])
                            vsn = self.sV.inc(i)
                            sy.wait_ge(self.sV.h, vsn)
                            self.dmaS(rt_tiles[ti, :, :], rt2[:, sub * D:(sub + 1) * D])
                        self.barrier()
        self.barrier()
        self.tap("resid2", self.dram["resid_hbm"].ap()[:, :, :])

    def emit_ln_f32(self, src_tile_ap, lnf_out, xbuf, stat, absmax_col):
        nc = self.nc
        v, s = nc.vector, nc.scalar
        xc, sqs = self.ln_xc, self.ln_sq
        self.nc.sync.wait_ge(self.sV.h, self.sV.n)
        dn = self.dmaS(xbuf, src_tile_ap)
        v.wait_ge(self.sD.h, dn)
        v.wait_ge(self.sS.h, self.sS.n)
        self.vs(v.tensor_reduce(stat[:, 0:1], xbuf, axis=AX.X, op=ALU.add))
        self.vs(v.tensor_scalar(stat[:, 1:2], stat[:, 0:1], -1.0 / D, None, ALU.mult))
        i = v.tensor_scalar(xc, xbuf, stat[:, 1:2], None, ALU.add)
        self.sV.inc(i)
        s.wait_ge(self.sV.h, self.sV.n)
        self.ss(s.activation(sqs, xc, AF.Square, accum_out=stat[:, 2:3]))
        i = s.activation(stat[:, 3:4], stat[:, 2:3], AF.Sqrt, bias=self.consts[:, 1:2],
                         scale=1.0 / D)
        self.sS.inc(i)
        v.wait_ge(self.sS.h, self.sS.n)
        self.vs(v.reciprocal(stat[:, 4:5], stat[:, 3:4]))
        self.vs(v.tensor_scalar(lnf_out, xc, stat[:, 4:5], None, ALU.mult))
        self.vs(v.tensor_reduce(stat[:, 5:6], lnf_out, axis=AX.X, op=ALU.max,
                                apply_absolute_value=True))
        self.vs(v.tensor_max(absmax_col, absmax_col, stat[:, 5:6]))

    def l1_attn(self):
        nc = self.nc
        v, s, t, sy, g = nc.vector, nc.scalar, nc.tensor, nc.sync, nc.gpsimd
        rt_tiles = self.dram["resid_hbm"].ap()
        with nc.sbuf_tensor("lnT1", [128, DCH, NSEQ], BF16) as lnT1_h:
            lnT1 = lnT1_h.ap()
            with nc.sbuf_tensor("l1_xb", [128, 2 * D], F32) as xb_h, \
                 nc.sbuf_tensor("l1_lb", [128, 2 * D], BF16) as lb_h, \
                 nc.sbuf_tensor("l1_xc", [128, D], F32) as xc_h, \
                 nc.sbuf_tensor("l1_sq", [128, D], F32) as sq_h2, \
                 nc.sbuf_tensor("l1_st", [128, 2 * 16], F32) as st_h:
                xb, lb, st = xb_h.ap(), lb_h.ap(), st_h.ap()
                self.ln_xc, self.ln_sq = xc_h.ap(), sq_h2.ap()
                for ti in range(NT):
                    p = ti % 2
                    self.emit_ln(rt_tiles[ti, :, :], lnT1, ti * 128,
                                 xb[:, p * D:(p + 1) * D], lb[:, p * D:(p + 1) * D],
                                 st[:, p * 16:(p + 1) * 16])
            self.emit_allgather_ln(lnT1, "cc_ln1_src", "cc_ln1_dst")
            ngrp = 0 if "l1ln" in self.stages else (1 if "l1g1" in self.stages else 2)
            for grp in range(ngrp):
                self.l1_group(grp, lnT1)
        self.barrier()
        self.tap("y2t", self.dram["y2t_hbm"].ap()[:, :, :])
        if "l1ln" in self.stages or "l1g1" in self.stages or "l1nop" in self.stages:
            return
        with nc.sbuf_tensor("wpn", [128, 8, D], BF16) as wpn_h, \
             nc.sbuf_tensor("wpT", [128, 8, D], BF16) as wpT_h, \
             nc.sbuf_tensor("y2s", [128, 2, 8, 128], BF16) as y2s_h, \
             nc.sbuf_tensor("rt1", [128, 2 * D], F32) as rt1_h:
            wpn, wpT, y2s, rt1 = wpn_h.ap(), wpT_h.ap(), y2s_h.ap(), rt1_h.ap()
            src = self.in_["wproj"].rearrange("(c p) f -> p c f", p=128)
            gn = self.dmaG(wpn[:, :, :], src[:, :, :])
            t.wait_ge(self.sG.h, gn)
            ev = self.sV.n
            for fc in range(8):
                for oc in range(8):
                    bank = self.pb[6 + (oc % 2)]
                    pv = bank.ap()[:, 0:64].bitcast(BF16)
                    t.wait_ge(self.sV.h, ev)
                    i = t.transpose(pv[:, 0:128], wpn[:, oc, fc * 128:(fc + 1) * 128],
                                    self.ident_b[:, :])
                    self.sT.inc(i)
                    v.wait_ge(self.sT.h, self.sT.n)
                    i = v.tensor_copy(wpT[:, fc, oc * 128:(oc + 1) * 128], pv[:, 0:128])
                    ev = self.sV.inc(i)
            self.barrier()
            y2_tiles = self.dram["y2t_hbm"].ap()
            evac_snap = [0, 0]
            store_snap = [0, 0]
            for ti in range(NT):
                pg = ti % 2
                sy.wait_ge(self.sD.h, store_snap[pg])
                dn = self.dmaS(y2s[:, pg, :, :],
                               y2_tiles[:, :, ti * 128:(ti + 1) * 128]
                               .rearrange("c p k -> p c k"))
                b0 = self.pb[pg * 2]
                b1 = self.pb[pg * 2 + 1]
                t.wait_ge(self.sD.h, dn)
                t.wait_ge(self.sV.h, self.sV.n if ti < 2 else evac_snap[pg])
                for fc in range(8):
                    st0 = (fc == 0)
                    sp = (fc == 7)
                    t.matmul(b0.ap()[:, 0:512], y2s[:, pg, fc, :], wpT[:, fc, 0:512],
                             start=st0, stop=sp)
                    i = t.matmul(b1.ap()[:, 0:512], y2s[:, pg, fc, :], wpT[:, fc, 512:1024],
                                 start=st0, stop=sp)
                self.sT.inc(i)
                tsnap = self.sT.n
                dn = self.dmaS(rt1[:, pg * D:pg * D + D], rt_tiles[ti, :, :])
                v.wait_ge(self.sD.h, dn)
                v.wait_ge(self.sT.h, tsnap)
                v.tensor_add(rt1[:, pg * D:pg * D + 512], b0.ap()[:, 0:512],
                             rt1[:, pg * D:pg * D + 512])
                i = v.tensor_add(rt1[:, pg * D + 512:pg * D + 1024], b1.ap()[:, 0:512],
                                 rt1[:, pg * D + 512:pg * D + 1024])
                evac_snap[pg] = self.sV.inc(i)
                sy.wait_ge(self.sV.h, evac_snap[pg])
                store_snap[pg] = self.dmaS(rt_tiles[ti, :, :], rt1[:, pg * D:pg * D + D])
            sy.wait_ge(self.sD.h, self.sD.n)
        self.barrier()
        self.tap("resid3", self.dram["resid_hbm"].ap()[:, :, :])

    def l1_group(self, grp, lnT1):
        nc = self.nc
        v, s, t, sy, g = nc.vector, nc.scalar, nc.tensor, nc.sync, nc.gpsimd
        with nc.sbuf_tensor(f"wTg{grp}", [128, DCH, 512], BF16) as qw_h, \
             nc.sbuf_tensor(f"wnl{grp}", [128, 2, D], BF16) as wn_h, \
             nc.sbuf_tensor(f"qhT{grp}", [128, 4, TOK], BF16) as qh_h, \
             nc.sbuf_tensor(f"khT{grp}", [128, 4, NSEQ], BF16) as kh_h, \
             nc.sbuf_tensor(f"vAg{grp}", [128, 32, 8 * 65], BF16) as vg_h, \
             nc.sbuf_tensor(f"tk{grp}", [128, 512], F32) as tk_h, \
             nc.sbuf_tensor(f"tk2{grp}", [128, 512], F32) as tk2_h, \
             nc.sbuf_tensor(f"tkb{grp}", [128, 512], BF16) as tkb_h, \
             nc.sbuf_tensor(f"trm{grp}", [128, 20], F32) as trm_h, \
             nc.sbuf_tensor(f"Pb1{grp}", [128, 2, 1024], BF16) as pb1_h, \
             nc.sbuf_tensor(f"otb{grp}", [65, 1024], F32) as otb_h, \
             nc.sbuf_tensor(f"y2g{grp}", [128, NT, 512], BF16) as y2g_h:
            wTg, wnl = qw_h.ap(), wn_h.ap()
            qhT, khT, vAg = qh_h.ap(), kh_h.ap(), vg_h.ap()
            tk, tk2, tkb, trm = tk_h.ap(), tk2_h.ap(), tkb_h.ap(), trm_h.ap()
            Pb1, otb, y2g = pb1_h.ap(), otb_h.ap(), y2g_h.ap()
            wsrc = self.in_["wqkv"].rearrange("(c p) d -> c p d", p=128)

            def load_wslice(wi):
                base = wi * 8 + grp * 4
                for hh in range(2):
                    t.wait_ge(self.sV.h, self.sV.n)
                    g.wait_ge(self.sT.h, self.sT.n)
                    gn = self.dmaG(wnl[:, :, :],
                                   wsrc[base + hh * 2:base + hh * 2 + 2, :, :]
                                   .rearrange("c p d -> p c d"))
                    t.wait_ge(self.sG.h, gn)
                    ev = self.sV.n
                    for fc in range(2):
                        for dch in range(DCH):
                            bank = self.pb[6 + (dch % 2)]
                            pv = bank.ap()[:, 0:64].bitcast(BF16)
                            t.wait_ge(self.sV.h, ev)
                            i = t.transpose(pv[:, 0:128],
                                            wnl[:, fc, dch * 128:(dch + 1) * 128],
                                            self.ident_b[:, :])
                            self.sT.inc(i)
                            v.wait_ge(self.sT.h, self.sT.n)
                            i = v.tensor_copy(
                                wTg[:, dch, (hh * 2 + fc) * 128:(hh * 2 + fc) * 128 + 128],
                                pv[:, 0:128])
                            ev = self.sV.inc(i)

            if "l1w" in self.stages:
                return
            for which, wi_, dst, ntile in (("q", 0, qhT, NT), ("k", 1, khT, 32)):
                wT = wTg
                t.wait_ge(self.sV.h, self.sV.n)
                self.barrier()
                load_wslice(wi_)
                for ti in range(ntile):
                    bank = self.pb[4]
                    t.wait_ge(self.sV.h, self.sV.n)
                    for dch in range(DCH):
                        i = t.matmul(bank.ap()[:, 0:512],
                                     lnT1[:, dch, ti * 128:(ti + 1) * 128],
                                     wT[:, dch, 0:512],
                                     start=(dch == 0), stop=(dch == DCH - 1))
                    self.sT.inc(i)
                    v.wait_ge(self.sT.h, self.sT.n)
                    self.vs(v.tensor_copy(tk[:, :], bank.ap()[:, 0:512]))
                    self.vs(v.tensor_mul(tk2[:, :], tk[:, :], tk[:, :]))
                    tv = tk2[:, :].rearrange("p (h e) -> p h e", e=64)
                    self.vs(v.tensor_reduce(trm[:, 0:8], tv, axis=AX.X, op=ALU.add))
                    s.wait_ge(self.sV.h, self.sV.n)
                    i = s.activation(trm[:, 8:16], trm[:, 0:8], AF.Sqrt,
                                     bias=self.consts[:, 2:3], scale=1.0 / 64.0)
                    self.sS.inc(i)
                    v.wait_ge(self.sS.h, self.sS.n)
                    self.vs(v.reciprocal(trm[:, 0:8], trm[:, 8:16]))
                    rb = trm[:, 0:8].rearrange("p (h o) -> p h o", o=1).broadcast_to(
                        [128, 8, 64])
                    self.vs(v.tensor_tensor(tkb[:, :].rearrange("p (h e) -> p h e", e=64),
                                            tv if False else
                                            tk[:, :].rearrange("p (h e) -> p h e", e=64),
                                            rb, ALU.mult))
                    ev3 = self.sV.n
                    for fc in range(4):
                        bank2 = self.pb[6 + (fc % 2)]
                        pv = bank2.ap()[:, 0:64].bitcast(BF16)
                        t.wait_ge(self.sV.h, ev3)
                        i = t.transpose(pv[:, 0:128], tkb[:, fc * 128:(fc + 1) * 128],
                                        self.ident_b[:, :])
                        self.sT.inc(i)
                        v.wait_ge(self.sT.h, self.sT.n)
                        i = v.tensor_copy(dst[:, fc, ti * 128:(ti + 1) * 128], pv[:, 0:128])
                        ev3 = self.vs(i)
            if "l1qk" in self.stages:
                return
            self.barrier()
            load_wslice(2)
            vwT = wTg
            for ti in range(32):
                bank = self.pb[4]
                t.wait_ge(self.sV.h, self.sV.n)
                for dch in range(DCH):
                    i = t.matmul(bank.ap()[:, 0:512],
                                 lnT1[:, dch, ti * 128:(ti + 1) * 128],
                                 vwT[:, dch, 0:512],
                                 start=(dch == 0), stop=(dch == DCH - 1))
                self.sT.inc(i)
                v.wait_ge(self.sT.h, self.sT.n)
                src3 = bank.ap()[:, 0:512].rearrange("p (h e) -> p h e", e=64)
                dst3 = vAg[:, ti, 0:520].rearrange("p (h e) -> p h e", e=65)[:, :, 0:64]
                i = v.tensor_copy(dst3, src3)
                self.vs(i)
            i = v.memset(vAg[:, :, :].rearrange("p t (h e) -> p t h e", e=65)[:, :, :, 64:65],
                         1.0)
            self.vs(i)
            if grp == 0:
                self.tap("qhT", qhT[:, :, :])
                self.tap("khT", khT[:, :, :])
                self.tap("vAg", vAg[:, :, :])
            if "l1v" in self.stages:
                return
            self.barrier()
            nhl = 1 if "l1h1" in self.stages else 8
            for hl in range(nhl):
                pbase = (hl * 64) % 128
                fch = (hl * 64) // 128
                t.wait_ge(self.sV.h, self.sV.n)
                s.wait_ge(self.sV.h, self.sV.n)
                exp_done = {}
                for kt in range(32):
                    for half in range(2):
                        u = kt * 2 + half
                        sb = self.pb[u % 2 * 2:u % 2 * 2 + 2]
                        if u >= 2:
                            t.wait_ge(self.sS.h, exp_done[u - 2])
                        for qq in range(2):
                            i = t.matmul(
                                sb[qq].ap()[:, 0:512],
                                khT[pbase:pbase + 64, fch, kt * 128:(kt + 1) * 128],
                                qhT[pbase:pbase + 64, fch,
                                    half * 1024 + qq * 512:half * 1024 + qq * 512 + 512],
                                start=True, stop=True)
                        tsnap = self.sT.inc(i)
                        s.wait_ge(self.sT.h, tsnap)
                        s.activation(Pb1[:, u % 2, 0:512], sb[0].ap()[:, 0:512],
                                     AF.Exp, bias=self.consts[:, 6:7], scale=SC1)
                        i = s.activation(Pb1[:, u % 2, 512:1024], sb[1].ap()[:, 0:512],
                                         AF.Exp, bias=self.consts[:, 6:7], scale=SC1)
                        exp_done[u] = self.sS.inc(i)
                        t.wait_ge(self.sS.h, exp_done[u])
                        for qq in range(2):
                            i = t.matmul(
                                self.pb[4 + half * 2 + qq].ap()[0:65, 0:512],
                                vAg[:, kt, hl * 65:(hl + 1) * 65],
                                Pb1[:, u % 2, qq * 512:(qq + 1) * 512],
                                start=(kt == 0), stop=(kt == 31))
                        self.sT.inc(i)
                v.wait_ge(self.sT.h, self.sT.n)
                for halfq in range(2):
                  if True:
                    for j in range(2):
                        i = v.tensor_copy(otb[0:65, j * 512:(j + 1) * 512],
                                          self.pb[4 + halfq * 2 + j].ap()[0:65, 0:512])
                    self.vs(i)
                    t.wait_ge(self.sV.h, self.sV.n)
                    for ti in range(halfq * 8, halfq * 8 + 8):
                        tloc = ti - halfq * 8
                        bank = self.pb[(ti % 2)]
                        t.wait_ge(self.sV.h, self.sV.n)
                        i = t.transpose(bank.ap()[:, 0:65],
                                        otb[0:65, tloc * 128:(tloc + 1) * 128],
                                        self.ident_f[0:65, 0:65])
                        ts_ = self.sT.inc(i)
                        v.wait_ge(self.sT.h, ts_)
                        ot = bank.ap()[:, 0:65]
                        vown = vAg[:, ti, hl * 65:hl * 65 + 64]
                        self.vs(v.reciprocal(trm[:, 0:1], ot[:, 64:65]))
                        self.vs(v.tensor_scalar(tk[:, 0:64], ot[:, 0:64], trm[:, 0:1],
                                                None, ALU.mult))
                        self.vs(v.tensor_tensor(tk[:, 64:128], vown, vown, ALU.mult))
                        self.vs(v.tensor_reduce(trm[:, 1:2], tk[:, 64:128], axis=AX.X,
                                                op=ALU.add))
                        s.wait_ge(self.sV.h, self.sV.n)
                        i = s.activation(trm[:, 2:3], trm[:, 1:2], AF.Sqrt,
                                         bias=self.consts[:, 3:4], scale=1.0)
                        self.sS.inc(i)
                        v.wait_ge(self.sS.h, self.sS.n)
                        self.vs(v.tensor_scalar(trm[:, 2:3], trm[:, 2:3], 1e-12, None,
                                                ALU.max))
                        self.vs(v.reciprocal(trm[:, 3:4], trm[:, 2:3]))
                        self.vs(v.tensor_scalar(tk[:, 128:192], vown, trm[:, 3:4], None,
                                                ALU.mult))
                        self.vs(v.tensor_tensor(tk[:, 192:256], tk[:, 0:64],
                                                tk[:, 128:192], ALU.mult))
                        self.vs(v.tensor_reduce(trm[:, 4:5], tk[:, 192:256], axis=AX.X,
                                                op=ALU.add))
                        self.vs(v.tensor_scalar(trm[:, 5:6], trm[:, 4:5], -1.0, None,
                                                ALU.mult))
                        i = v.scalar_tensor_tensor(
                            y2g[:, ti, hl * 64:(hl + 1) * 64], tk[:, 128:192],
                            trm[:, 5:6], tk[:, 0:64], ALU.mult, ALU.add)
                        self.vs(i)
            if "l1noy2" in self.stages:
                return
            for ti in range(NT):
                ev2 = None
                v.wait_ge(self.sD.h, self.sD.n)
                for fc in range(4):
                    bank = self.pb[6 + (fc % 2)]
                    pv = bank.ap()[:, 0:64].bitcast(BF16)
                    t.wait_ge(self.sV.h, self.sV.n)
                    i = t.transpose(pv[:, 0:128], y2g[:, ti, fc * 128:(fc + 1) * 128],
                                    self.ident_b[:, :])
                    self.sT.inc(i)
                    v.wait_ge(self.sT.h, self.sT.n)
                    i = v.tensor_copy(self.gtev[:, fc % 2, :], pv[:, 0:128])
                    ev2 = self.sV.inc(i)
                    sy.wait_ge(self.sV.h, ev2)
                    self.dmaS(self.dram["y2t_hbm"].ap()[4 * grp + fc, :,
                                                        ti * 128:(ti + 1) * 128],
                              self.gtev[:, fc % 2, :])
                    v.wait_ge(self.sD.h, self.sD.n)

    def moe(self):
        nc = self.nc
        v, s, t, sy, g = nc.vector, nc.scalar, nc.tensor, nc.sync, nc.gpsimd
        rt_tiles = self.dram["resid_hbm"].ap()
        with nc.sbuf_tensor("lnTm", [128, DCH, TOK], BF16) as lnTm_h:
            lnTm = lnTm_h.ap()
            with nc.sbuf_tensor("m_xb", [128, 2 * D], F32) as xb_h, \
                 nc.sbuf_tensor("m_lb", [128, 2 * D], BF16) as lb_h, \
                 nc.sbuf_tensor("m_xc", [128, D], F32) as xc_h, \
                 nc.sbuf_tensor("m_sq", [128, D], F32) as sq_h2, \
                 nc.sbuf_tensor("m_st", [128, 2 * 16], F32) as st_h:
                xb, lb, st = xb_h.ap(), lb_h.ap(), st_h.ap()
                self.ln_xc, self.ln_sq = xc_h.ap(), sq_h2.ap()
                for ti in range(NT):
                    p = ti % 2
                    self.emit_ln(rt_tiles[ti, :, :], lnTm, ti * 128,
                                 xb[:, p * D:(p + 1) * D], lb[:, p * D:(p + 1) * D],
                                 st[:, p * 16:(p + 1) * 16])
            self.barrier()
            gn = self.dmaG(self.dram["cc_mln_src"].ap()[:, :], lnTm[:, :, 0:TOK])
            g.wait_ge(self.sG.h, gn)
            i = g.collective_compute("AllGather", ALU.bypass,
                                     replica_groups=[list(range(NCORES))],
                                     ins=[self.dram["cc_mln_src"].ap().opt()],
                                     outs=[self.dram["cc_mln_dst"].ap().opt()])
            self.sC.inc(i)
            g.wait_ge(self.sC.h, self.sC.n)
        with nc.sbuf_tensor("moeln", [128, DCH, 512], BF16) as moeln_h, \
             nc.sbuf_tensor("hT", [128, 64, 512], BF16) as hT_h, \
             nc.sbuf_tensor("w1c", [128, 2, DCH, 512], BF16) as w1c_h, \
             nc.sbuf_tensor("w2c", [128, 2, 8, D], BF16) as w2c_h, \
             nc.sbuf_tensor("sew", [128, 4], F32) as sew_h, \
             nc.sbuf_tensor("ybuf", [128, 4 * D], F32) as ybuf_h:
            moeln, hT, w1c, w2c = moeln_h.ap(), hT_h.ap(), w1c_h.ap(), w2c_h.ap()
            sew, ybuf = sew_h.ap(), ybuf_h.ap()
            pid2 = nc.partition_id(engines=[mybir.EngineType.Pool])
            dstv = self.dram["cc_mln_dst"].ap().rearrange("r p (c k) -> r p c k", c=DCH)
            for slot in range(2):
                gn = self.dmaG(moeln[:, :, slot * 256:(slot + 1) * 256],
                               dstv[slot, :, :, bass.ds(pid2 * 256, 256)])
            g.wait_ge(self.sG.h, gn)
            with nc.sbuf_tensor("ew", [128, 8], F32) as ew_h:
                ew = ew_h.ap()
                esrc = self.in_["expw"].rearrange("(q p) k -> p q k", p=128)
                dn = self.dmaS(ew[:, 0:8], esrc[:, :, :])
                v.wait_ge(self.sD.h, dn)
                i = v.tensor_tensor(sew[:, 0:4], ew[:, 0:8:2], ew[:, 1:8:2], ALU.add)
                self.vs(i)
            self.barrier()
            w1src = self.in_["w1"].rearrange("(c p) i -> p c i", p=128)
            wload = {0: self.dmaG(w1c[:, 0, :, :], w1src[:, :, 0:512])}
            for mc in range(16):
                if mc + 1 < 16:
                    if mc >= 1:
                        g.wait_ge(self.sT.h, self.mc_done[mc - 1])
                    wload[mc + 1] = self.dmaG(
                        w1c[:, (mc + 1) % 2, :, :],
                        w1src[:, :, (mc + 1) * 512:(mc + 2) * 512])
                t.wait_ge(self.sG.h, wload[mc])
                for ic4 in range(4):
                    bank = self.pb[ic4 % 2]
                    for dch in range(DCH):
                        i = t.matmul(bank.ap()[:, 0:512],
                                     w1c[:, mc % 2, dch, ic4 * 128:(ic4 + 1) * 128],
                                     moeln[:, dch, 0:512],
                                     start=(dch == 0), stop=(dch == DCH - 1))
                    tsnap = self.sT.inc(i)
                    s.wait_ge(self.sT.h, tsnap)
                    i = s.activation(hT[:, mc * 4 + ic4, :], bank.ap()[:, 0:512],
                                     AF.Gelu_apprx_tanh)
                    ssnap = self.sS.inc(i)
                    t.wait_ge(self.sS.h, ssnap)
                self.mc_done = getattr(self, "mc_done", {})
                self.mc_done[mc] = self.sT.n
            self.barrier()
            w2src = self.in_["w2"].rearrange("(c p) d -> p c d", p=128)
            w2load = {0: self.dmaG(w2c[:, 0, :, :], w2src[:, 0:8, :])}
            for wb in range(8):
                if wb + 1 < 8:
                    if wb >= 1:
                        g.wait_ge(self.sT.h, self.wb_done[wb - 1])
                    w2load[wb + 1] = self.dmaG(w2c[:, (wb + 1) % 2, :, :],
                                               w2src[:, (wb + 1) * 8:(wb + 2) * 8, :])
                t.wait_ge(self.sG.h, w2load[wb])
                for ic8 in range(8):
                    ic = wb * 8 + ic8
                    for sub in range(4):
                        for half in range(2):
                            i = t.matmul(self.pb[sub * 2 + half].ap()[:, 0:512],
                                         hT[:, ic, sub * 128:(sub + 1) * 128],
                                         w2c[:, wb % 2, ic8, half * 512:(half + 1) * 512],
                                         start=(ic == 0), stop=(ic == 63))
                self.wb_done = getattr(self, "wb_done", {})
                self.wb_done[wb] = self.sT.inc(i)
            v.wait_ge(self.sT.h, self.sT.n)
            for sub in range(4):
                v.tensor_scalar(ybuf[:, sub * D:sub * D + 512],
                                self.pb[sub * 2].ap()[:, 0:512], sew[:, sub:sub + 1],
                                None, ALU.mult)
                i = v.tensor_scalar(ybuf[:, sub * D + 512:sub * D + 1024],
                                    self.pb[sub * 2 + 1].ap()[:, 0:512],
                                    sew[:, sub:sub + 1], None, ALU.mult)
            vsn = self.sV.inc(i)
            sy.wait_ge(self.sV.h, vsn)
            self.dmaS(self.dram["cc_mret_src"].ap().rearrange("(q p) d -> p q d", p=128),
                      ybuf[:, :].rearrange("p (q d) -> p q d", d=D))
        self.barrier()
        i = g.collective_compute("AllGather", ALU.bypass,
                                 replica_groups=[list(range(NCORES))],
                                 ins=[self.dram["cc_mret_src"].ap().opt()],
                                 outs=[self.dram["cc_mret_dst"].ap().opt()])
        self.sC.inc(i)
        g.wait_ge(self.sC.h, self.sC.n)
        self.barrier()
        with nc.sbuf_tensor("mret", [128, 2, D], F32) as mret_h, \
             nc.sbuf_tensor("rtf", [128, 2, D], F32) as rtf_h:
            mret, rtf = mret_h.ap(), rtf_h.ap()
            pid3 = nc.partition_id(engines=[mybir.EngineType.SP])
            off = nc.s_assert_within((pid3 % 2) * 256, min_val=0, max_val=256,
                                     skip_runtime_assert=True)
            retv = self.dram["cc_mret_dst"].ap()
            gsn = {0: 0, 1: 0}
            for ti in range(NT):
                p = ti % 2
                sy.wait_ge(self.sV.h, self.sV.n)
                self.dmaS(mret[:, p, :],
                          retv[ti // 2, bass.ds(off + (ti % 2) * 128, 128), :])
                dn = self.dmaS(rtf[:, p, :], rt_tiles[ti, :, :])
                v.wait_ge(self.sD.h, dn)
                i = v.scalar_tensor_tensor(rtf[:, p, :], mret[:, p, :],
                                           self.moe_gate_col[:, 0:1], rtf[:, p, :],
                                           ALU.mult, ALU.add)
                vsn2 = self.sV.inc(i)
                sy.wait_ge(self.sV.h, vsn2)
                self.dmaS(self.out[ti, :, :], rtf[:, p, :])
        self.barrier()

    def build(self, stages=("l0a", "l0f", "l1a", "moe")):
        self.stages = stages
        nc = self.nc
        with nc.Block():
            self.prologue()
            self.gtev = nc.alloc_sbuf_tensor("gtev", [128, 2, 128], BF16).ap()
            if "l0a" in self.stages:
                self.l0_attn()
            if "l0f" in self.stages:
                self.l0_ffn()
            if "l1a" in self.stages:
                self.l1_attn()
            if "moe" in self.stages:
                self.moe()
            sy = nc.sync
            self.wait_all(sy)
            if "moe" not in self.stages:
                for ti in range(NT):
                    self.dmaS(self.out[ti, :, :], self.dram["resid_hbm"].ap()[ti, :, :])
            sy.wait_ge(self.sD.h, self.sD.n)
        return nc


def _ownfirst(c):
    p = c % 2
    own = np.concatenate([np.arange(256 * (2 * j + p), 256 * (2 * j + p) + 256)
                          for j in range(8)])
    peer = np.concatenate([np.arange(256 * (2 * j + 1 - p), 256 * (2 * j + 1 - p) + 256)
                           for j in range(8)])
    return own, peer


def _build_masks(c):
    p = c % 2
    m = np.ones((8, 4, 128, 256), np.float32)
    for qc in range(8):
        qa = 256 * (2 * qc + p) + np.arange(256)
        for i in range(4):
            rng, blk = divmod(i, 2)
            blk += 2 * qc
            if rng == 0:
                kb = 256 * (2 * (blk // 2) + p) + (blk % 2) * 128
            else:
                kb = 256 * (2 * (blk // 2) + 1 - p) + (blk % 2) * 128
            ka = kb + np.arange(128)
            m[qc, i] = (ka[:, None] <= qa[None, :]).astype(np.float32)
    return np.ascontiguousarray(
        m.transpose(2, 0, 1, 3).reshape(128, 8 * 4 * 256)).astype(BFnp)


_NC_CACHE = {}


def _get_nc(taps=(), stages=("l0a", "l0f", "l1a", "moe")):
    key = (tuple(sorted(taps)), tuple(stages))
    if key not in _NC_CACHE:
        _NC_CACHE[key] = KB(debug_taps=taps).build(stages)
    return _NC_CACHE[key]


def kernel(**inputs):
    return _run(inputs, taps=())[0]


def _run(inputs, taps=(), stages=("l0a", "l0f", "l1a", "moe")):
    nc = _get_nc(taps, stages)
    X = np.ascontiguousarray(np.asarray(inputs["L0_attn_X"], dtype=np.float32))
    base = {
        "wq": np.ascontiguousarray(inputs["L0_attn_W_q"], dtype=np.float32),
        "wk": np.ascontiguousarray(inputs["L0_attn_W_k"], dtype=np.float32),
        "wv": np.ascontiguousarray(inputs["L0_attn_W_v"], dtype=np.float32),
        "wo": np.ascontiguousarray(inputs["L0_attn_W_o"], dtype=np.float32),
        "lq1": np.ascontiguousarray(inputs["L0_attn_lambda_q1"], dtype=np.float32),
        "lk1": np.ascontiguousarray(inputs["L0_attn_lambda_k1"], dtype=np.float32),
        "lq2": np.ascontiguousarray(inputs["L0_attn_lambda_q2"], dtype=np.float32),
        "lk2": np.ascontiguousarray(inputs["L0_attn_lambda_k2"], dtype=np.float32),
        "rms_scale": np.ascontiguousarray(inputs["L0_attn_rms_scale"], dtype=np.float32),
        "wgate": np.ascontiguousarray(inputs["L0_ffn_w_gate"], dtype=np.float32),
        "wup": np.ascontiguousarray(inputs["L0_ffn_w_up"], dtype=np.float32),
        "wdown": np.ascontiguousarray(inputs["L0_ffn_w_down"], dtype=np.float32),
        "wqkv": np.ascontiguousarray(inputs["L1_attn_w_qkv"], dtype=np.float32),
        "wproj": np.ascontiguousarray(inputs["L1_attn_w_proj"], dtype=np.float32),
        "w1": np.ascontiguousarray(inputs["L1_ffn_w1"], dtype=np.float32),
        "w2": np.ascontiguousarray(inputs["L1_ffn_w2"], dtype=np.float32),
    }
    expw = np.asarray(inputs["L1_ffn_expert_weights"], dtype=np.float32)
    in_maps = []
    for c in range(NCORES):
        own, _ = _ownfirst(c)
        m = dict(base)
        m["x_own"] = np.ascontiguousarray(X[c // 2][own])
        m["expw"] = np.ascontiguousarray(expw[512 * c:512 * (c + 1)])
        m["masks"] = _build_masks(c)
        m["moe_gate"] = np.full((128, 1), 1.0 if c < 2 else 0.0, np.float32)
        in_maps.append(m)
    res = run_bass_kernel_spmd(nc, in_maps, core_ids=list(range(NCORES)))
    out = np.zeros((4, NSEQ, D), np.float32)
    for c in range(NCORES):
        own, _ = _ownfirst(c)
        out[c // 2][own] = res.results[c]["out"].reshape(TOK, D)
    return out, res


# version 05
# speedup vs baseline: 1.1000x; 1.1000x over previous
import numpy as np
import ml_dtypes
import concourse.bass as bass
import concourse.mybir as mybir
from concourse.bass_utils import run_bass_kernel_spmd

F32 = mybir.dt.float32
BF16 = mybir.dt.bfloat16
AF = mybir.ActivationFunctionType
ALU = mybir.AluOpType
AX = mybir.AxisListType
BFnp = ml_dtypes.bfloat16

NCORES = 8
TOK = 2048
NT = 16
NSEQ = 4096
D = 1024
DCH = 8
H0 = 8
H1 = 16
MAGIC = 12582912.0
QMAX = 32767.0
LAM_INIT = 0.8
SC0 = float(1.0 / np.sqrt(128.0))
SC1 = 0.125
INTER = 3072
ICH = 24
MINTER = 8192
RMS_EPS1 = 1.1920928955078125e-07


class Sem:
    def __init__(self, nc, name):
        self.h = nc.alloc_semaphore(name)
        self.n = 0

    def inc(self, instr, k=1):
        instr.then_inc(self.h, k)
        self.n += k
        return self.n


class KB:
    def __init__(self, debug_taps=()):
        self.taps = set(debug_taps)
        nc = self.nc = bass.Bass("TRN2", target_bir_lowering=False, debug=False)
        self.in_ = {}
        for name, shape, dt in [
            ("x_own", [TOK, D], F32),
            ("wq", [D, 2048], F32), ("wk", [D, 2048], F32), ("wv", [D, 2048], F32),
            ("wo", [2048, D], F32),
            ("lq1", [8, 128], F32), ("lk1", [8, 128], F32),
            ("lq2", [8, 128], F32), ("lk2", [8, 128], F32),
            ("rms_scale", [256], F32),
            ("wgate", [INTER, D], F32), ("wup", [INTER, D], F32), ("wdown", [D, INTER], F32),
            ("wqkv", [3 * D, D], F32), ("wproj", [D, D], F32),
            ("w1", [D, MINTER], F32), ("w2", [MINTER, D], F32),
            ("expw", [512, 2], F32),
            ("masks", [128, 8 * 4 * 256], BF16),
            ("moe_gate", [128, 1], F32),
        ]:
            self.in_[name] = nc.dram_tensor(name, shape, dt, kind="ExternalInput").ap()
        self.out = nc.dram_tensor("out", [NT, 128, D], F32, kind="ExternalOutput").ap()

        self.dram = {}
        for name, shape, dt in [
            ("cc_ln0_src", [128, DCH * TOK], BF16),
            ("cc_ln0_dst", [NCORES, 128, DCH * TOK], BF16),
            ("cc_ln1_src", [128, DCH * TOK], BF16),
            ("cc_ln1_dst", [NCORES, 128, DCH * TOK], BF16),
            ("cc_mx_src", [1, 16], F32),
            ("cc_mx_dst", [1, 16], F32),
            ("cc_mln_src", [128, DCH * TOK], BF16),
            ("cc_mln_dst", [NCORES, 128, DCH * TOK], BF16),
            ("cc_mret_src", [512, D], F32),
            ("cc_mret_dst", [NCORES, 512, D], F32),
            ("sp_up", [ICH, 128, TOK], F32),
            ("sp_gate", [ICH, 128, TOK], F32),
            ("sp_t", [ICH, 128, TOK], F32),
            ("resid_hbm", [NT, 128, D], F32),
            ("gt_hbm", [16, 128, TOK], BF16),
            ("y2t_hbm", [8, 128, TOK], BF16),
        ]:
            self.dram[name] = nc.dram_tensor(name, shape, dt, kind="Internal")

        self.sD = Sem(nc, "sD")
        self.sG = Sem(nc, "sG")
        self.sC = Sem(nc, "sC")
        self.sT = Sem(nc, "sT")
        self.sV = Sem(nc, "sV")
        self.sS = Sem(nc, "sS")
        self.ALL = (self.sD, self.sG, self.sC, self.sT, self.sV, self.sS)
        self.pb = [nc.alloc_psum_tensor(f"pb{i}", [128, 512], F32) for i in range(8)]

    def wait_all(self, eng):
        for s in self.ALL:
            if s.n:
                eng.wait_ge(s.h, s.n)

    def barrier(self):
        nc = self.nc
        for eng in (nc.sync, nc.gpsimd, nc.tensor, nc.vector, nc.scalar):
            self.wait_all(eng)

    def vs(self, i):
        n = self.sV.inc(i)
        self.nc.vector.wait_ge(self.sV.h, n)
        return n

    def ss(self, i):
        n = self.sS.inc(i)
        self.nc.scalar.wait_ge(self.sS.h, n)
        return n

    def dmaS(self, out, in_):
        i = self.nc.sync.dma_start(out=out, in_=in_)
        self.sD.inc(i, 16)
        return self.sD.n

    def dmaG(self, out, in_):
        i = self.nc.gpsimd.dma_start(out=out, in_=in_)
        self.sG.inc(i, 16)
        return self.sG.n

    def tap(self, name, ap, shape=None, dt=None):
        if name not in self.taps:
            return
        shape = shape or list(ap.shape)
        t = self.nc.dram_tensor("tap_" + name, shape, dt or ap.dtype, kind="ExternalOutput").ap()
        self.barrier()
        self.dmaS(t[...], ap)
        self.nc.sync.wait_ge(self.sD.h, self.sD.n)

    def bcast128(self, dst_col, src11):
        nc = self.nc
        t, v = nc.tensor, nc.vector
        t.wait_ge(self.sV.h, self.sV.n)
        i = t.matmul(self.pb[7].ap()[:, 0:1], self.ones_f[0:1, 0:128], src11,
                     start=True, stop=True)
        self.sT.inc(i)
        v.wait_ge(self.sT.h, self.sT.n)
        self.vs(v.tensor_copy(dst_col, self.pb[7].ap()[:, 0:1]))

    def part_max(self, dst11, src_col):
        nc = self.nc
        t, v = nc.tensor, nc.vector
        t.wait_ge(self.sV.h, self.sV.n)
        i = t.transpose(self.pb[7].ap()[0:1, 0:128], src_col, self.ident_f[:, :])
        self.sT.inc(i)
        v.wait_ge(self.sT.h, self.sT.n)
        self.vs(v.tensor_reduce(dst11, self.pb[7].ap()[0:1, 0:128], axis=AX.X, op=ALU.max))

    def allreduce_max(self, idx):
        nc = self.nc
        g = nc.gpsimd
        self.barrier()
        gn = self.dmaG(self.dram["cc_mx_src"].ap()[0:1, :], self.mxA[0:1, :])
        g.wait_ge(self.sG.h, gn)
        i = g.collective_compute("AllReduce", ALU.max, replica_groups=[list(range(NCORES))],
                                 ins=[self.dram["cc_mx_src"].ap().opt()],
                                 outs=[self.dram["cc_mx_dst"].ap().opt()])
        self.sC.inc(i)
        g.wait_ge(self.sC.h, self.sC.n)
        gn = self.dmaG(self.mxA[0:1, :], self.dram["cc_mx_dst"].ap()[0:1, :])
        g.wait_ge(self.sG.h, gn)
        self.barrier()

    def prologue(self):
        nc = self.nc
        a = nc.alloc_sbuf_tensor
        self.ident_f = a("ident_f", [128, 128], F32).ap()
        self.ident_b = a("ident_b", [128, 128], BF16).ap()
        self.ones_f = a("ones_f", [128, 128], F32).ap()
        self.ones_b = a("ones_b", [128, 1], BF16).ap()
        self.consts = a("consts", [128, 8], F32).ap()
        self.rms2 = a("rms2", [128, 2], F32).ap()
        self.lamb = a("lamb", [128, H0], F32).ap()
        self.mxA = a("mxA", [1, 16], F32).ap()
        self.moe_gate_col = a("moegate", [128, 1], F32).ap()
        self.row = a("rowbuf", [1, 4096], F32).ap()
        self.smax_col = a("smax", [128, 1], F32).ap()
        eye_f = nc.inline_tensor(np.eye(128, dtype=np.float32), name="eye_f")
        eye_b = nc.inline_tensor(np.eye(128, dtype=np.float32).astype(BFnp), name="eye_b")
        self.dmaS(self.ident_f[:, :], eye_f.ap()[:, :])
        self.dmaS(self.ident_b[:, :], eye_b.ap()[:, :])
        self.dmaS(self.rms2[:, 0:1], self.in_["rms_scale"][0:128][:, None])
        self.dmaS(self.rms2[:, 1:2], self.in_["rms_scale"][128:256][:, None])
        self.dmaS(self.moe_gate_col[:, :], self.in_["moe_gate"][:, :])
        v, s, t = nc.vector, nc.scalar, nc.tensor
        v.memset(self.ones_f[:, :], 1.0)
        v.memset(self.ones_b[:, :], 1.0)
        v.memset(self.consts[:, 0:1], 1.0)
        v.memset(self.consts[:, 1:2], 1e-5)
        v.memset(self.consts[:, 2:3], RMS_EPS1)
        v.memset(self.consts[:, 3:4], 0.0)
        v.memset(self.consts[:, 4:5], 1e-12)
        v.memset(self.consts[:, 5:6], MAGIC)
        v.memset(self.consts[:, 6:7], -8.0)
        i = v.memset(self.mxA[0:1, :], 0.0)
        self.sV.inc(i)
        lt = a("lam_t", [8, 512], F32).ap()
        self.lr = lr = a("lam_r", [8, 8], F32).ap()
        self.dmaS(lt[0:8, 0:128], self.in_["lq1"][:, :])
        self.dmaS(lt[0:8, 128:256], self.in_["lk1"][:, :])
        self.dmaS(lt[0:8, 256:384], self.in_["lq2"][:, :])
        self.dmaS(lt[0:8, 384:512], self.in_["lk2"][:, :])
        v.wait_ge(self.sD.h, self.sD.n)
        v.tensor_mul(lt[0:8, 0:128], lt[0:8, 0:128], lt[0:8, 128:256])
        self.vs(v.tensor_mul(lt[0:8, 256:384], lt[0:8, 256:384], lt[0:8, 384:512]))
        v.tensor_reduce(lr[0:8, 0:1], lt[0:8, 0:128], axis=AX.X, op=ALU.add)
        i = v.tensor_reduce(lr[0:8, 1:2], lt[0:8, 256:384], axis=AX.X, op=ALU.add)
        self.sV.inc(i)
        s.wait_ge(self.sV.h, self.sV.n)
        i = s.activation(lr[0:8, 2:4], lr[0:8, 0:2], AF.Exp)
        self.sS.inc(i)
        v.wait_ge(self.sS.h, self.sS.n)
        self.vs(v.tensor_sub(lr[0:8, 0:1], lr[0:8, 2:3], lr[0:8, 3:4]))
        i = v.tensor_scalar(lr[0:8, 0:1], lr[0:8, 0:1], LAM_INIT, None, ALU.add)
        self.sV.inc(i)
        t.wait_ge(self.sV.h, self.sV.n)
        i = t.transpose(self.pb[7].ap()[0:1, 0:8], lr[0:8, 0:1], self.ident_f[0:8, 0:8])
        self.sT.inc(i)
        v.wait_ge(self.sT.h, self.sT.n)
        i = v.tensor_copy(self.row[0:1, 0:8], self.pb[7].ap()[0:1, 0:8])
        self.sV.inc(i)
        t.wait_ge(self.sV.h, self.sV.n)
        i = t.matmul(self.pb[7].ap()[:, 0:8], self.ones_f[0:1, 0:128], self.row[0:1, 0:8],
                     start=True, stop=True)
        self.sT.inc(i)
        v.wait_ge(self.sT.h, self.sT.n)
        i = v.tensor_copy(self.lamb[:, :], self.pb[7].ap()[:, 0:8])
        self.sV.inc(i)

    def emit_ln(self, src_tile_ap, lnT, col_base, xbuf, lnb, stat, absmax_col=None,
                lnf=None):
        nc = self.nc
        v, s, t = nc.vector, nc.scalar, nc.tensor
        xc, sqs = self.ln_xc, self.ln_sq
        self.nc.sync.wait_ge(self.sV.h, self.sV.n)
        dn = self.dmaS(xbuf, src_tile_ap)
        v.wait_ge(self.sD.h, dn)
        v.wait_ge(self.sS.h, self.sS.n)
        self.vs(v.tensor_reduce(stat[:, 0:1], xbuf, axis=AX.X, op=ALU.add))
        self.vs(v.tensor_scalar(stat[:, 1:2], stat[:, 0:1], -1.0 / D, None, ALU.mult))
        i = v.tensor_scalar(xc, xbuf, stat[:, 1:2], None, ALU.add)
        self.sV.inc(i)
        s.wait_ge(self.sV.h, self.sV.n)
        self.ss(s.activation(sqs, xc, AF.Square, accum_out=stat[:, 2:3]))
        i = s.activation(stat[:, 3:4], stat[:, 2:3], AF.Sqrt, bias=self.consts[:, 1:2],
                         scale=1.0 / D)
        self.sS.inc(i)
        v.wait_ge(self.sS.h, self.sS.n)
        self.vs(v.reciprocal(stat[:, 4:5], stat[:, 3:4]))
        tgt = lnf if lnf is not None else lnb
        v.wait_ge(self.sT.h, self.sT.n)
        self.vs(v.tensor_scalar(tgt, xc, stat[:, 4:5], None, ALU.mult))
        if lnf is not None:
            if absmax_col is not None:
                self.vs(v.tensor_reduce(stat[:, 5:6], lnf, axis=AX.X, op=ALU.max,
                                        apply_absolute_value=True))
                self.vs(v.tensor_max(absmax_col, absmax_col, stat[:, 5:6]))
            i = v.tensor_copy(lnb, lnf)
            self.sV.inc(i)
        ev = self.sV.n
        for dch in range(DCH):
            bank = self.pb[6 + (dch % 2)]
            pv = bank.ap()[:, 0:64].bitcast(BF16)
            t.wait_ge(self.sV.h, ev)
            i = t.transpose(pv[:, 0:128], lnb[:, dch * 128:(dch + 1) * 128], self.ident_b[:, :])
            self.sT.inc(i)
            v.wait_ge(self.sT.h, self.sT.n)
            i = v.tensor_copy(lnT[:, dch, col_base:col_base + 128], pv[:, 0:128])
            ev = self.sV.inc(i)

    def emit_ln_phase(self, src_tiles, lnT, absmax_col=None):
        nc = self.nc
        with nc.sbuf_tensor("ln_xb", [128, 2 * D], F32) as xb_h, \
             nc.sbuf_tensor("ln_lb", [128, 2 * D], BF16) as lb_h, \
             nc.sbuf_tensor("ln_lf", [128, D], F32) as lf_h, \
             nc.sbuf_tensor("ln_xc", [128, D], F32) as xc_h, \
             nc.sbuf_tensor("ln_sq", [128, D], F32) as sq_h2, \
             nc.sbuf_tensor("ln_st", [128, 2 * 16], F32) as st_h:
            xb, lb, st = xb_h.ap(), lb_h.ap(), st_h.ap()
            lf = lf_h.ap()
            self.ln_xc, self.ln_sq = xc_h.ap(), sq_h2.ap()
            for ti in range(NT):
                p = ti % 2
                self.emit_ln(src_tiles[ti, :, :], lnT, ti * 128,
                             xb[:, p * D:(p + 1) * D], lb[:, p * D:(p + 1) * D],
                             st[:, p * 16:(p + 1) * 16],
                             absmax_col=absmax_col,
                             lnf=(lf[:, :] if absmax_col is not None else None))

    def emit_allgather_ln(self, lnT, src_name, dst_name):
        nc = self.nc
        g = nc.gpsimd
        self.barrier()
        gn = self.dmaG(self.dram[src_name].ap()[:, :], lnT[:, :, 0:TOK])
        g.wait_ge(self.sG.h, gn)
        i = g.collective_compute("AllGather", ALU.bypass, replica_groups=[list(range(NCORES))],
                                 ins=[self.dram[src_name].ap().opt()],
                                 outs=[self.dram[dst_name].ap().opt()])
        self.sC.inc(i)
        g.wait_ge(self.sC.h, self.sC.n)
        pid = nc.partition_id(engines=[mybir.EngineType.Pool])
        peer = self.nc.s_assert_within(pid + 1 - 2 * (pid % 2), min_val=0,
                                       max_val=NCORES - 1, skip_runtime_assert=True)
        dst_view = self.dram[dst_name].ap().rearrange("r p (c k) -> r p c k", c=DCH)
        gn = self.dmaG(lnT[:, :, TOK:NSEQ], dst_view[bass.ds(peer, 1), :, :, :])
        g.wait_ge(self.sG.h, gn)
        self.barrier()

    def l0_attn(self):
        nc = self.nc
        v, s, t, sy = nc.vector, nc.scalar, nc.tensor, nc.sync
        x_tiles = self.in_["x_own"].rearrange("(n p) d -> n p d", p=128)
        with nc.sbuf_tensor("masks_sb", [128, 8 * 4 * 256], BF16) as masks_h, \
             nc.sbuf_tensor("lnT0", [128, DCH, NSEQ], BF16) as lnT_h:
            masks, lnT = masks_h.ap(), lnT_h.ap()
            self.dmaS(masks[:, :], self.in_["masks"][:, :])
            self.emit_ln_phase(x_tiles, lnT)
            self.tap("lnT_own", lnT[:, :, 0:TOK])
            self.emit_allgather_ln(lnT, "cc_ln0_src", "cc_ln0_dst")
            self.tap("lnT_all", lnT[:, :, :])
            with nc.sbuf_tensor("wh", [128, 2, DCH, 768], BF16) as wh_h, \
                 nc.sbuf_tensor("kT", [128, 2, NSEQ], BF16) as kT_h, \
                 nc.sbuf_tensor("qT", [128, 2, TOK], BF16) as qT_h, \
                 nc.sbuf_tensor("vA", [128, 32, 257], BF16) as vA_h, \
                 nc.sbuf_tensor("Pb", [128, 4, 256], BF16) as Pb_h, \
                 nc.sbuf_tensor("hsc", [128, 16], F32) as hsc_h, \
                 nc.sbuf_tensor("sqc", [128, 512], BF16) as sq_h, \
                 nc.sbuf_tensor("nrm", [1, NSEQ], F32) as nrm_h, \
                 nc.sbuf_tensor("gbuf", [128, 768], F32) as gb_h, \
                 nc.sbuf_tensor("gb16", [128, 256], BF16) as gb16_h:
                ctx = dict(wh=wh_h.ap(), kT=kT_h.ap(), qT=qT_h.ap(), vA=vA_h.ap(),
                           Pb=Pb_h.ap(), hsc=hsc_h.ap(), sqc=sq_h.ap(), nrm=nrm_h.ap(),
                           gbuf=gb_h.ap(), gb16=gb16_h.ap(), masks=masks, lnT=lnT)
                wsnap = {0: self.load_l0_head_w(0, ctx)}
                self.proj_done = {}
                for h in range(H0):
                    if h + 1 < H0:
                        if h >= 1:
                            nc.gpsimd.wait_ge(self.sT.h, self.proj_done[h - 1])
                        wsnap[h + 1] = self.load_l0_head_w(h + 1, ctx)
                    self.l0_head(h, ctx, wsnap[h])
        self.barrier()
        with nc.sbuf_tensor("wof", [128, 16, D], F32) as wof_h, \
             nc.sbuf_tensor("wob", [128, 16, D], BF16) as wob_h, \
             nc.sbuf_tensor("gts", [128, 2, 16, 128], BF16) as gts_h, \
             nc.sbuf_tensor("rt", [128, 2 * D], F32) as rt_h:
            wof, wob, gts, rt = wof_h.ap(), wob_h.ap(), gts_h.ap(), rt_h.ap()
            wsrc = self.in_["wo"].rearrange("(c p) d -> p c d", p=128)
            dn = self.dmaS(wof[:, :, :], wsrc[:, :, :])
            v.wait_ge(self.sD.h, dn)
            for ch in range(16):
                i = v.tensor_scalar(wob[:, ch, :], wof[:, ch, :],
                                    self.rms2[:, (ch % 2):(ch % 2) + 1], 0.2,
                                    ALU.mult, ALU.mult)
            self.sV.inc(i)
            gt_tiles = self.dram["gt_hbm"].ap()
            evac_snap = [0, 0]
            store_snap = [0, 0]
            for ti in range(NT):
                pg = ti % 2
                sy.wait_ge(self.sD.h, store_snap[pg])
                dn = self.dmaS(gts[:, pg, :, :],
                               gt_tiles[:, :, ti * 128:(ti + 1) * 128]
                               .rearrange("c p k -> p c k"))
                b0 = self.pb[pg * 2]
                b1 = self.pb[pg * 2 + 1]
                t.wait_ge(self.sD.h, dn)
                t.wait_ge(self.sV.h, self.sV.n if ti < 2 else evac_snap[pg])
                for fc in range(16):
                    st0 = (fc == 0)
                    sp = (fc == 15)
                    t.matmul(b0.ap()[:, 0:512], gts[:, pg, fc, :], wob[:, fc, 0:512],
                             start=st0, stop=sp)
                    i = t.matmul(b1.ap()[:, 0:512], gts[:, pg, fc, :], wob[:, fc, 512:1024],
                                 start=st0, stop=sp)
                self.sT.inc(i)
                tsnap = self.sT.n
                dn = self.dmaS(rt[:, pg * D:pg * D + D], x_tiles[ti, :, :])
                v.wait_ge(self.sD.h, dn)
                v.wait_ge(self.sT.h, tsnap)
                v.tensor_add(rt[:, pg * D:pg * D + 512], b0.ap()[:, 0:512],
                             rt[:, pg * D:pg * D + 512])
                i = v.tensor_add(rt[:, pg * D + 512:pg * D + 1024], b1.ap()[:, 0:512],
                                 rt[:, pg * D + 512:pg * D + 1024])
                evac_snap[pg] = self.sV.inc(i)
                sy.wait_ge(self.sV.h, evac_snap[pg])
                store_snap[pg] = self.dmaS(self.dram["resid_hbm"].ap()[ti, :, :],
                                           rt[:, pg * D:pg * D + D])
            sy.wait_ge(self.sD.h, self.sD.n)
        self.barrier()
        self.tap("resid1", self.dram["resid_hbm"].ap()[:, :, :])
        self.tap("gt", self.dram["gt_hbm"].ap()[:, :, :])

    def load_l0_head_w(self, h, ctx):
        wh = ctx["wh"]
        sl = wh[:, h % 2]
        n = 0
        for j, nm in enumerate(("wq", "wk", "wv")):
            src = self.in_[nm].rearrange("(c p) f -> p c f", p=128)
            n = self.dmaG(sl[:, :, j * 256:(j + 1) * 256], src[:, :, h * 256:(h + 1) * 256])
        return n

    def l0_head(self, h, ctx, wsnap):
        nc = self.nc
        v, s, t = nc.vector, nc.scalar, nc.tensor
        wh, kT, qT, vA = ctx["wh"], ctx["kT"], ctx["qT"], ctx["vA"]
        Pb, hsc, sqc, nrm = ctx["Pb"], ctx["hsc"], ctx["sqc"], ctx["nrm"]
        gbuf, gb16, masks, lnT = ctx["gbuf"], ctx["gb16"], ctx["masks"], ctx["lnT"]
        whh = wh[:, h % 2]
        t.wait_ge(self.sG.h, wsnap)
        bank_ev = {}

        def proj(bank_i, lhsT, rhs, ncol, dst):
            bank = self.pb[bank_i]
            if bank_i in bank_ev:
                t.wait_ge(self.sV.h, bank_ev[bank_i])
            for dch in range(DCH):
                i = t.matmul(bank.ap()[:, 0:ncol], lhsT(dch), rhs(dch),
                             start=(dch == 0), stop=(dch == DCH - 1))
            self.sT.inc(i)
            v.wait_ge(self.sT.h, self.sT.n)
            i = v.tensor_copy(dst, bank.ap()[:, 0:ncol])
            bank_ev[bank_i] = self.sV.inc(i)

        for st_i in range(2):
            for ck in range(8):
                proj(ck % 2,
                     lambda dch, st_i=st_i: whh[:, dch, 256 + st_i * 128:256 + st_i * 128 + 128],
                     lambda dch, ck=ck: lnT[:, dch, ck * 512:(ck + 1) * 512],
                     512, kT[:, st_i, ck * 512:(ck + 1) * 512])
            for cq in range(4):
                proj(2 + cq % 2,
                     lambda dch, st_i=st_i: whh[:, dch, st_i * 128:st_i * 128 + 128],
                     lambda dch, cq=cq: lnT[:, dch, cq * 512:(cq + 1) * 512],
                     512, qT[:, st_i, cq * 512:(cq + 1) * 512])
        for kt in range(32):
            proj(4 + kt % 2,
                 lambda dch, kt=kt: lnT[:, dch, kt * 128:(kt + 1) * 128],
                 lambda dch: whh[:, dch, 512:768],
                 256, vA[:, kt, 0:256])
        self.proj_done[h] = self.sT.n
        i = v.memset(vA[:, :, 256:257], 1.0)
        self.sV.inc(i)
        for st_i in range(2):
            for which, src, ncols, dst_i in ((0, kT, 8, 8 + st_i), (1, qT, 4, 10 + st_i)):
                for ck in range(ncols):
                    i = v.tensor_mul(sqc[:, 0:512], src[:, st_i, ck * 512:(ck + 1) * 512],
                                     src[:, st_i, ck * 512:(ck + 1) * 512])
                    vs = self.sV.inc(i)
                    t.wait_ge(self.sV.h, vs)
                    i = t.matmul(self.pb[6].ap()[0:1, 0:512], self.ones_b[:, 0:1],
                                 sqc[:, 0:512], start=True, stop=True)
                    ts_ = self.sT.inc(i)
                    v.wait_ge(self.sT.h, ts_)
                    i = v.tensor_copy(nrm[0:1, ck * 512:(ck + 1) * 512],
                                      self.pb[6].ap()[0:1, 0:512])
                    self.vs(i)
                self.vs(v.tensor_reduce(hsc[0:1, dst_i:dst_i + 1], nrm[0:1, 0:ncols * 512],
                                        axis=AX.X, op=ALU.max))
            i = v.tensor_mul(hsc[0:1, 12 + st_i:13 + st_i], hsc[0:1, 8 + st_i:9 + st_i],
                             hsc[0:1, 10 + st_i:11 + st_i])
            self.sV.inc(i)
            s.wait_ge(self.sV.h, self.sV.n)
            i = s.activation(hsc[0:1, 14 + st_i:15 + st_i], hsc[0:1, 12 + st_i:13 + st_i],
                             AF.Sqrt, bias=self.consts[0:1, 3:4], scale=1.0)
            self.sS.inc(i)
            v.wait_ge(self.sS.h, self.sS.n)
            self.vs(v.tensor_scalar(hsc[0:1, 14 + st_i:15 + st_i],
                                    hsc[0:1, 14 + st_i:15 + st_i],
                                    -SC0 * 1.00001, None, ALU.mult))
            self.bcast128(hsc[:, st_i:st_i + 1], hsc[0:1, 14 + st_i:15 + st_i])
        if h == 0:
            self.tap("h0_kT", kT[:, :, :])
            self.tap("h0_qT", qT[:, :, :])
            self.tap("h0_vA", vA[:, :, :])
            self.tap("h0_hsc", hsc[:, :])
        s.wait_ge(self.sV.h, self.sV.n)
        for qc in range(8):
            nkt = 2 * qc + 2
            units = [(rng, kt) for rng in range(2) for kt in range(nkt)]
            nu = len(units)
            exp_snap = {}
            ready = {}
            for u in range(nu + 1):
                if u < nu:
                    rng, kt = units[u]
                    sb = self.pb[u % 2]
                    if u >= 2:
                        t.wait_ge(self.sS.h, exp_snap[u - 2])
                    kbase = rng * TOK + kt * 128
                    t.matmul(sb.ap()[:, 0:256], kT[:, 0, kbase:kbase + 128],
                             qT[:, 0, qc * 256:(qc + 1) * 256], start=True, stop=True)
                    i = t.matmul(sb.ap()[:, 256:512], kT[:, 1, kbase:kbase + 128],
                                 qT[:, 1, qc * 256:(qc + 1) * 256], start=True, stop=True)
                    tsnap = self.sT.inc(i)
                    s.wait_ge(self.sT.h, tsnap)
                    s.activation(Pb[:, (u % 2) * 2, :], sb.ap()[:, 0:256], AF.Exp,
                                 bias=hsc[:, 0:1], scale=SC0)
                    i = s.activation(Pb[:, (u % 2) * 2 + 1, :], sb.ap()[:, 256:512], AF.Exp,
                                     bias=hsc[:, 1:2], scale=SC0)
                    exp_snap[u] = self.sS.inc(i)
                    if kt >= 2 * qc:
                        mi = qc * 4 + rng * 2 + (kt - 2 * qc)
                        v.wait_ge(self.sS.h, exp_snap[u])
                        v.tensor_mul(Pb[:, (u % 2) * 2, :], Pb[:, (u % 2) * 2, :],
                                     masks[:, mi * 256:(mi + 1) * 256])
                        i = v.tensor_mul(Pb[:, (u % 2) * 2 + 1, :], Pb[:, (u % 2) * 2 + 1, :],
                                         masks[:, mi * 256:(mi + 1) * 256])
                        ready[u] = ("V", self.sV.inc(i))
                    else:
                        ready[u] = ("S", exp_snap[u])
                if u > 0:
                    rng0, kt0 = units[u - 1]
                    kind, val = ready[u - 1]
                    t.wait_ge(self.sS.h if kind == "S" else self.sV.h, val)
                    kti = rng0 * 16 + kt0
                    st0 = (u - 1 == 0)
                    sp = (u - 1 == nu - 1)
                    for st_i in range(2):
                        for sub in range(2):
                            i = t.matmul(self.pb[2 + st_i * 2 + sub].ap()[:, 0:257],
                                         Pb[:, ((u - 1) % 2) * 2 + st_i,
                                            sub * 128:(sub + 1) * 128],
                                         vA[:, kti, 0:257], start=st0, stop=sp)
                    self.sT.inc(i)
            v.wait_ge(self.sT.h, self.sT.n)
            for sub in range(2):
                o1 = self.pb[2 + sub].ap()
                o2 = self.pb[4 + sub].ap()
                v.reciprocal(hsc[:, 4:5], o1[:, 256:257])
                self.vs(v.reciprocal(hsc[:, 5:6], o2[:, 256:257]))
                v.tensor_scalar(hsc[:, 6:7], hsc[:, 5:6], self.lamb[:, h:h + 1], -1.0,
                                ALU.mult, ALU.mult)
                self.vs(v.tensor_scalar(gbuf[:, 0:256], o1[:, 0:256], hsc[:, 4:5], None,
                                        ALU.mult))
                self.vs(v.scalar_tensor_tensor(gbuf[:, 256:512], o2[:, 0:256], hsc[:, 6:7],
                                               gbuf[:, 0:256], ALU.mult, ALU.add))
                self.vs(v.tensor_mul(gbuf[:, 512:768], gbuf[:, 256:512], gbuf[:, 256:512]))
                i = v.tensor_reduce(hsc[:, 7:8], gbuf[:, 512:768], axis=AX.X, op=ALU.add)
                self.sV.inc(i)
                s.wait_ge(self.sV.h, self.sV.n)
                i = s.activation(hsc[:, 4:5], hsc[:, 7:8], AF.Sqrt, bias=self.consts[:, 1:2],
                                 scale=1.0 / 256.0)
                self.sS.inc(i)
                v.wait_ge(self.sS.h, self.sS.n)
                self.vs(v.reciprocal(hsc[:, 5:6], hsc[:, 4:5]))
                v.wait_ge(self.sT.h, self.sT.n)
                i = v.tensor_scalar(gb16[:, 0:256], gbuf[:, 256:512], hsc[:, 5:6], None,
                                    ALU.mult)
                vs = self.sV.inc(i)
                t.wait_ge(self.sV.h, vs)
                v.wait_ge(self.sD.h, self.sD.n)
                for fc in range(2):
                    bank = self.pb[6 + fc]
                    pv = bank.ap()[:, 0:64].bitcast(BF16)
                    i = t.transpose(pv[:, 0:128], gb16[:, fc * 128:(fc + 1) * 128],
                                    self.ident_b[:, :])
                    ts_ = self.sT.inc(i)
                    v.wait_ge(self.sT.h, ts_)
                    i = v.tensor_copy(self.gtev[:, fc, :], pv[:, 0:128])
                    vs2 = self.sV.inc(i)
                self.nc.sync.wait_ge(self.sV.h, vs2)
                self.dmaS(self.dram["gt_hbm"].ap()[2 * h:2 * h + 2, :,
                                                   qc * 256 + sub * 128:
                                                   qc * 256 + sub * 128 + 128]
                          .rearrange("c p k -> p c k"),
                          self.gtev[:, :, :])
            t.wait_ge(self.sV.h, self.sV.n)

    def qdq(self, out_ap, in_ap, rcol, scol, ncol):
        v = self.nc.vector
        mg = self.consts[:, 5:6].broadcast_to([128, ncol])
        self.vs(v.scalar_tensor_tensor(self.qtmp[:, 0:ncol], in_ap, rcol, mg,
                                       ALU.mult, ALU.add))
        self.vs(v.tensor_scalar(out_ap, self.qtmp[:, 0:ncol], -MAGIC, scol,
                                ALU.add, ALU.mult))

    def mk_scale(self, idx, s_col, r_col, pre=1.0 / QMAX, rmul=1.0):
        nc = self.nc
        v = nc.vector
        v.wait_ge(self.sG.h, self.sG.n)
        self.vs(v.tensor_scalar(self.mxA[0:1, 8:9], self.mxA[0:1, idx:idx + 1],
                                pre, 1e-12, ALU.mult, ALU.max))
        self.vs(v.reciprocal(self.mxA[0:1, 9:10], self.mxA[0:1, 8:9]))
        if rmul != 1.0:
            self.vs(v.tensor_scalar(self.mxA[0:1, 9:10], self.mxA[0:1, 9:10],
                                    rmul, None, ALU.mult))
        self.bcast128(s_col, self.mxA[0:1, 8:9])
        self.bcast128(r_col, self.mxA[0:1, 9:10])

    def dequant_tile(self, wq_bf, wnat, dq, nblk):
        v = self.nc.vector
        ncol = nblk * 32
        wv = wnat.rearrange("p (g k) -> p g k", k=32)
        self.vs(v.tensor_reduce(dq[:, 0:nblk], wv, axis=AX.X, op=ALU.max,
                                apply_absolute_value=True))
        self.vs(v.tensor_scalar(dq[:, 0:nblk], dq[:, 0:nblk], 1.0 / 7.0, 1e-12,
                                ALU.mult, ALU.max))
        self.vs(v.reciprocal(dq[:, nblk:2 * nblk], dq[:, 0:nblk]))
        rb = dq[:, nblk:2 * nblk].rearrange("p (g o) -> p g o", o=1).broadcast_to(
            [128, nblk, 32])
        sb = dq[:, 0:nblk].rearrange("p (g o) -> p g o", o=1).broadcast_to(
            [128, nblk, 32])
        qv = self.qtmp[:, 0:ncol].rearrange("p (g k) -> p g k", k=32)
        self.vs(v.tensor_tensor(qv, wv, rb, ALU.mult))
        self.vs(v.tensor_scalar(self.qtmp[:, 0:ncol], self.qtmp[:, 0:ncol], MAGIC, -MAGIC,
                                ALU.add, ALU.add))
        self.vs(v.tensor_tensor(wq_bf.rearrange("p (g k) -> p g k", k=32), qv, sb,
                                ALU.mult))

    def l0_ffn(self):
        nc = self.nc
        v, s, t, sy, g = nc.vector, nc.scalar, nc.tensor, nc.sync, nc.gpsimd
        rt_tiles = self.dram["resid_hbm"].ap()
        with nc.sbuf_tensor("lnT2", [128, DCH, TOK], BF16) as lnT2_h, \
             nc.sbuf_tensor("qtmp", [128, 3072], F32) as qtmp_h, \
             nc.sbuf_tensor("scol", [128, 12], F32) as scol_h:
            lnT2 = lnT2_h.ap()
            self.qtmp = qtmp_h.ap()
            scol = scol_h.ap()
            i = v.memset(self.mxA[0:1, 0:8], 0.0)
            self.vs(i)
            i = v.memset(self.smax_col, 0.0)
            self.vs(i)
            with nc.sbuf_tensor("ln2f", [128, NT, D], F32) as ln2f_h, \
                 nc.sbuf_tensor("ln2_xb", [128, 2 * D], F32) as xb_h, \
                 nc.sbuf_tensor("ln2_xc", [128, D], F32) as xc_h, \
                 nc.sbuf_tensor("ln2_sq", [128, D], F32) as sq_h2, \
                 nc.sbuf_tensor("ln2_st", [128, 2 * 16], F32) as st_h:
                ln2f, xb, st = ln2f_h.ap(), xb_h.ap(), st_h.ap()
                self.ln_xc, self.ln_sq = xc_h.ap(), sq_h2.ap()
                for ti in range(NT):
                    p = ti % 2
                    self.emit_ln_f32(rt_tiles[ti, :, :], ln2f[:, ti, :],
                                     xb[:, p * D:(p + 1) * D], st[:, p * 16:(p + 1) * 16],
                                     self.smax_col)
                self.part_max(self.mxA[0:1, 0:1], self.smax_col)
                self.allreduce_max(0)
                self.mk_scale(0, scol[:, 0:1], scol[:, 1:2])
                with nc.sbuf_tensor("lnb2", [128, D], BF16) as lnb2_h:
                    lnb2 = lnb2_h.ap()
                    for ti in range(NT):
                        v.wait_ge(self.sT.h, self.sT.n)
                        self.qdq(lnb2[:, :], ln2f[:, ti, :], scol[:, 1:2], scol[:, 0:1], D)
                        ev = self.sV.n
                        for dch in range(DCH):
                            bank = self.pb[6 + (dch % 2)]
                            pv = bank.ap()[:, 0:64].bitcast(BF16)
                            t.wait_ge(self.sV.h, ev)
                            i = t.transpose(pv[:, 0:128], lnb2[:, dch * 128:(dch + 1) * 128],
                                            self.ident_b[:, :])
                            self.sT.inc(i)
                            v.wait_ge(self.sT.h, self.sT.n)
                            i = v.tensor_copy(lnT2[:, dch, ti * 128:(ti + 1) * 128],
                                              pv[:, 0:128])
                            ev = self.sV.inc(i)
            with nc.sbuf_tensor("wupT", [128, DCH, INTER], BF16) as wupT_h, \
                 nc.sbuf_tensor("wgT", [128, DCH, INTER], BF16) as wgT_h, \
                 nc.sbuf_tensor("wnat", [128, 2 * D], F32) as wnat_h, \
                 nc.sbuf_tensor("wdq", [128, 64], F32) as wdq_h, \
                 nc.sbuf_tensor("wqb", [128, D], BF16) as wqb_h:
                wupT, wgT = wupT_h.ap(), wgT_h.ap()
                wnat, wdq, wqb = wnat_h.ap(), wdq_h.ap(), wqb_h.ap()
                for wt, nm in ((wupT, "wup"), (wgT, "wgate")):
                    src = self.in_[nm].rearrange("(c p) d -> c p d", p=128)
                    ld = {}
                    for rt_i in range(ICH):
                        p = rt_i % 2
                        sy.wait_ge(self.sV.h, self.sV.n)
                        ld[rt_i] = self.dmaS(wnat[:, p * D:(p + 1) * D], src[rt_i, :, :])
                        v.wait_ge(self.sD.h, ld[rt_i])
                        v.wait_ge(self.sT.h, self.sT.n)
                        self.dequant_tile(wqb[:, :], wnat[:, p * D:(p + 1) * D], wdq, 32)
                        ev = self.sV.n
                        for dch in range(DCH):
                            bank = self.pb[6 + (dch % 2)]
                            pv = bank.ap()[:, 0:64].bitcast(BF16)
                            t.wait_ge(self.sV.h, ev)
                            i = t.transpose(pv[:, 0:128], wqb[:, dch * 128:(dch + 1) * 128],
                                            self.ident_b[:, :])
                            self.sT.inc(i)
                            v.wait_ge(self.sT.h, self.sT.n)
                            i = v.tensor_copy(wt[:, dch, rt_i * 128:(rt_i + 1) * 128],
                                              pv[:, 0:128])
                            ev = self.sV.inc(i)
                self.barrier()
                with nc.sbuf_tensor("ubuf", [128, 2 * 512], F32) as ubuf_h, \
                     nc.sbuf_tensor("mx2", [128, 4], F32) as mx2_h:
                    ubuf, mx2 = ubuf_h.ap(), mx2_h.ap()
                    i = v.memset(mx2[:, :], 0.0)
                    self.vs(i)
                    for wi, (wt, spname, mcol) in enumerate(
                            ((wupT, "sp_up", 0), (wgT, "sp_gate", 1))):
                        sp = self.dram[spname].ap()
                        st_snap = {}
                        for ic in range(ICH):
                            for tc in range(4):
                                u = ic * 4 + tc
                                bank = self.pb[u % 2]
                                for dch in range(DCH):
                                    i = t.matmul(bank.ap()[:, 0:512],
                                                 wt[:, dch, ic * 128:(ic + 1) * 128],
                                                 lnT2[:, dch, tc * 512:(tc + 1) * 512],
                                                 start=(dch == 0), stop=(dch == DCH - 1))
                                self.sT.inc(i)
                                pbuf = u % 2
                                v.wait_ge(self.sT.h, self.sT.n)
                                v.wait_ge(self.sD.h, st_snap.get(pbuf, 0))
                                i = v.tensor_copy(ubuf[:, pbuf * 512:(pbuf + 1) * 512],
                                                  bank.ap()[:, 0:512])
                                vs = self.vs(i)
                                self.vs(v.tensor_reduce(
                                    mx2[:, 2:3], ubuf[:, pbuf * 512:(pbuf + 1) * 512],
                                    axis=AX.X, op=ALU.max, apply_absolute_value=True))
                                self.vs(v.tensor_max(mx2[:, mcol:mcol + 1],
                                                     mx2[:, mcol:mcol + 1], mx2[:, 2:3]))
                                sy.wait_ge(self.sV.h, vs)
                                st_snap[pbuf] = self.dmaS(
                                    sp[ic, :, tc * 512:(tc + 1) * 512],
                                    ubuf[:, pbuf * 512:(pbuf + 1) * 512])
                                t.wait_ge(self.sV.h, vs)
                    self.part_max(self.mxA[0:1, 1:2], mx2[:, 0:1])
                    self.part_max(self.mxA[0:1, 2:3], mx2[:, 1:2])
                self.allreduce_max(1)
            self.mk_scale(1, scol[:, 2:3], scol[:, 3:4])
            self.mk_scale(2, scol[:, 4:5], scol[:, 5:6])
            with nc.sbuf_tensor("cbuf", [128, 6 * 512], F32) as cbuf_h, \
                 nc.sbuf_tensor("mx3", [128, 4], F32) as mx2_h:
                cbuf, mx2 = cbuf_h.ap(), mx2_h.ap()
                i = v.memset(mx2[:, :], 0.0)
                self.vs(i)
                spg = self.dram["sp_gate"].ap()
                spt = self.dram["sp_t"].ap()
                st_snap = {}
                for u in range(ICH * 4):
                    ic, tc = divmod(u, 4)
                    p = u % 2
                    gb = cbuf[:, (p * 3) * 512:(p * 3 + 1) * 512]
                    sgb = cbuf[:, (p * 3 + 1) * 512:(p * 3 + 2) * 512]
                    tb = cbuf[:, (p * 3 + 2) * 512:(p * 3 + 3) * 512]
                    sy.wait_ge(self.sD.h, st_snap.get(p, 0))
                    sy.wait_ge(self.sV.h, self.sV.n)
                    dn = self.dmaS(gb, spg[ic, :, tc * 512:(tc + 1) * 512])
                    v.wait_ge(self.sD.h, dn)
                    self.qdq(gb, gb, scol[:, 5:6], scol[:, 4:5], 512)
                    s.wait_ge(self.sV.h, self.sV.n)
                    i = s.activation(sgb, gb, AF.Sigmoid)
                    self.sS.inc(i)
                    v.wait_ge(self.sS.h, self.sS.n)
                    self.vs(v.tensor_scalar(sgb, sgb, 65536.0, MAGIC, ALU.mult, ALU.add))
                    self.vs(v.tensor_scalar(sgb, sgb, -MAGIC, 65535.0, ALU.add, ALU.min))
                    self.vs(v.tensor_tensor(tb, gb, sgb, ALU.mult))
                    self.vs(v.tensor_reduce(mx2[:, 2:3], tb, axis=AX.X, op=ALU.max,
                                            apply_absolute_value=True))
                    self.vs(v.tensor_max(mx2[:, 0:1], mx2[:, 0:1], mx2[:, 2:3]))
                    sy.wait_ge(self.sV.h, self.sV.n)
                    st_snap[p] = self.dmaS(spt[ic, :, tc * 512:(tc + 1) * 512], tb)
                    if u == 0:
                        self.tap("f3_gb", gb)
                        self.tap("f3_sgb", sgb)
                        self.tap("f3_mx", mx2[:, :])
                self.tap("f3_mx_end", mx2[:, :])
                self.part_max(self.mxA[0:1, 3:4], mx2[:, 0:1])
                self.vs(v.tensor_scalar(self.mxA[0:1, 3:4], self.mxA[0:1, 3:4],
                                        1.0 / 65536.0, None, ALU.mult))
                self.tap("f3_premax", self.mxA[0:1, :])
                self.allreduce_max(3)
                self.tap("f3_postmax", self.mxA[0:1, :])
                self.mk_scale(3, scol[:, 6:7], scol[:, 7:8], rmul=1.0 / 65536.0)
                i = v.memset(mx2[:, 0:2], 0.0)
                self.vs(i)
                spu = self.dram["sp_up"].ap()
                st_snap = {}
                for u in range(ICH * 4):
                    ic, tc = divmod(u, 4)
                    p = u % 2
                    g2 = cbuf[:, (p * 3) * 512:(p * 3 + 1) * 512]
                    ub = cbuf[:, (p * 3 + 1) * 512:(p * 3 + 2) * 512]
                    pr = cbuf[:, (p * 3 + 2) * 512:(p * 3 + 3) * 512]
                    sy.wait_ge(self.sD.h, st_snap.get(p, 0))
                    sy.wait_ge(self.sV.h, self.sV.n)
                    dn = self.dmaS(g2, spt[ic, :, tc * 512:(tc + 1) * 512])
                    dn = self.dmaS(ub, spu[ic, :, tc * 512:(tc + 1) * 512])
                    v.wait_ge(self.sD.h, dn)
                    if u == 0:
                        self.tap("f4_t_in", g2)
                        self.tap("f4_up_in", ub)
                    self.qdq(g2, g2, scol[:, 7:8], scol[:, 6:7], 512)
                    self.qdq(ub, ub, scol[:, 3:4], scol[:, 2:3], 512)
                    self.vs(v.tensor_tensor(pr, g2, ub, ALU.mult))
                    if u == 0:
                        self.tap("f4_g2", g2)
                        self.tap("f4_ub", ub)
                        self.tap("f4_scol", scol[:, :])
                    self.vs(v.tensor_reduce(mx2[:, 2:3], pr, axis=AX.X, op=ALU.max,
                                            apply_absolute_value=True))
                    self.vs(v.tensor_max(mx2[:, 0:1], mx2[:, 0:1], mx2[:, 2:3]))
                    sy.wait_ge(self.sV.h, self.sV.n)
                    st_snap[p] = self.dmaS(spg[ic, :, tc * 512:(tc + 1) * 512], pr)
                self.part_max(self.mxA[0:1, 4:5], mx2[:, 0:1])
                self.allreduce_max(4)
                self.mk_scale(4, scol[:, 8:9], scol[:, 9:10])
            self.tap("scol", scol[:, :])
            self.tap("mxa", self.mxA[0:1, :])
            self.tap("lnT2", lnT2[:, :, :])
            self.tap("spup", self.dram["sp_up"].ap()[0, :, :])
            self.tap("spt", self.dram["sp_t"].ap()[0, :, :])
            self.tap("spu2", self.dram["sp_gate"].ap()[0, :, :])
            with nc.sbuf_tensor("wdT", [128, ICH, D], BF16) as wdT_h, \
                 nc.sbuf_tensor("wnat_d", [128, 3072], F32) as wnat_h, \
                 nc.sbuf_tensor("wdq_d", [128, 192], F32) as wdq_h, \
                 nc.sbuf_tensor("wqb_d", [128, 3072], BF16) as wqb_h:
                wdT, wnat, wdq, wqb = wdT_h.ap(), wnat_h.ap(), wdq_h.ap(), wqb_h.ap()
                src = self.in_["wdown"].rearrange("(c p) i -> c p i", p=128)
                for rt_i in range(DCH):
                    sy.wait_ge(self.sV.h, self.sV.n)
                    dn = self.dmaS(wnat[:, :], src[rt_i, :, :])
                    v.wait_ge(self.sD.h, dn)
                    v.wait_ge(self.sT.h, self.sT.n)
                    self.dequant_tile(wqb[:, :], wnat[:, :], wdq, 96)
                    ev = self.sV.n
                    for ich in range(ICH):
                        bank = self.pb[6 + (ich % 2)]
                        pv = bank.ap()[:, 0:64].bitcast(BF16)
                        t.wait_ge(self.sV.h, ev)
                        i = t.transpose(pv[:, 0:128], wqb[:, ich * 128:(ich + 1) * 128],
                                        self.ident_b[:, :])
                        self.sT.inc(i)
                        v.wait_ge(self.sT.h, self.sT.n)
                        i = v.tensor_copy(wdT[:, ich, rt_i * 128:(rt_i + 1) * 128],
                                          pv[:, 0:128])
                        ev = self.sV.inc(i)
                self.barrier()
                with nc.sbuf_tensor("obuf", [128, 2 * 512], BF16) as obuf_h, \
                     nc.sbuf_tensor("ubuf2", [128, 2 * 512], F32) as ubuf2_h, \
                     nc.sbuf_tensor("rt2", [128, 4 * D], F32) as rt2_h:
                    obuf, ubuf2, rt2 = obuf_h.ap(), ubuf2_h.ap(), rt2_h.ap()
                    spg = self.dram["sp_gate"].ap()
                    for tg in range(4):
                        ld_snap = {}
                        for ic in range(ICH):
                            p = ic % 2
                            ub = ubuf2[:, p * 512:(p + 1) * 512]
                            ob = obuf[:, p * 512:(p + 1) * 512]
                            sy.wait_ge(self.sV.h, self.sV.n)
                            sy.wait_ge(self.sT.h, ld_snap.get(p, 0))
                            dn = self.dmaS(ub, spg[ic, :, tg * 512:(tg + 1) * 512])
                            v.wait_ge(self.sD.h, dn)
                            v.wait_ge(self.sT.h, ld_snap.get(p, 0))
                            self.qdq(ob, ub, scol[:, 9:10], scol[:, 8:9], 512)
                            t.wait_ge(self.sV.h, self.sV.n)
                            for sub in range(4):
                                for half in range(2):
                                    i = t.matmul(
                                        self.pb[sub * 2 + half].ap()[:, 0:512],
                                        ob[:, sub * 128:(sub + 1) * 128],
                                        wdT[:, ic, half * 512:(half + 1) * 512],
                                        start=(ic == 0), stop=(ic == ICH - 1))
                            ld_snap[p] = self.sT.inc(i)
                        tdone = self.sT.n
                        for sub in range(4):
                            ti = tg * 4 + sub
                            dn = self.dmaS(rt2[:, sub * D:(sub + 1) * D], rt_tiles[ti, :, :])
                            v.wait_ge(self.sD.h, dn)
                            v.wait_ge(self.sT.h, tdone)
                            v.tensor_add(rt2[:, sub * D:sub * D + 512],
                                         self.pb[sub * 2].ap()[:, 0:512],
                                         rt2[:, sub * D:sub * D + 512])
                            i = v.tensor_add(rt2[:, sub * D + 512:sub * D + 1024],
                                             self.pb[sub * 2 + 1].ap()[:, 0:512],
                                             rt2[:, sub * D + 512:sub * D + 1024])
                            vsn = self.sV.inc(i)
                            sy.wait_ge(self.sV.h, vsn)
                            self.dmaS(rt_tiles[ti, :, :], rt2[:, sub * D:(sub + 1) * D])
                        self.barrier()
        self.barrier()
        self.tap("resid2", self.dram["resid_hbm"].ap()[:, :, :])

    def emit_ln_f32(self, src_tile_ap, lnf_out, xbuf, stat, absmax_col):
        nc = self.nc
        v, s = nc.vector, nc.scalar
        xc, sqs = self.ln_xc, self.ln_sq
        self.nc.sync.wait_ge(self.sV.h, self.sV.n)
        dn = self.dmaS(xbuf, src_tile_ap)
        v.wait_ge(self.sD.h, dn)
        v.wait_ge(self.sS.h, self.sS.n)
        self.vs(v.tensor_reduce(stat[:, 0:1], xbuf, axis=AX.X, op=ALU.add))
        self.vs(v.tensor_scalar(stat[:, 1:2], stat[:, 0:1], -1.0 / D, None, ALU.mult))
        i = v.tensor_scalar(xc, xbuf, stat[:, 1:2], None, ALU.add)
        self.sV.inc(i)
        s.wait_ge(self.sV.h, self.sV.n)
        self.ss(s.activation(sqs, xc, AF.Square, accum_out=stat[:, 2:3]))
        i = s.activation(stat[:, 3:4], stat[:, 2:3], AF.Sqrt, bias=self.consts[:, 1:2],
                         scale=1.0 / D)
        self.sS.inc(i)
        v.wait_ge(self.sS.h, self.sS.n)
        self.vs(v.reciprocal(stat[:, 4:5], stat[:, 3:4]))
        self.vs(v.tensor_scalar(lnf_out, xc, stat[:, 4:5], None, ALU.mult))
        self.vs(v.tensor_reduce(stat[:, 5:6], lnf_out, axis=AX.X, op=ALU.max,
                                apply_absolute_value=True))
        self.vs(v.tensor_max(absmax_col, absmax_col, stat[:, 5:6]))

    def l1_attn(self):
        nc = self.nc
        v, s, t, sy, g = nc.vector, nc.scalar, nc.tensor, nc.sync, nc.gpsimd
        rt_tiles = self.dram["resid_hbm"].ap()
        with nc.sbuf_tensor("lnT1", [128, DCH, NSEQ], BF16) as lnT1_h:
            lnT1 = lnT1_h.ap()
            with nc.sbuf_tensor("l1_xb", [128, 2 * D], F32) as xb_h, \
                 nc.sbuf_tensor("l1_lb", [128, 2 * D], BF16) as lb_h, \
                 nc.sbuf_tensor("l1_xc", [128, D], F32) as xc_h, \
                 nc.sbuf_tensor("l1_sq", [128, D], F32) as sq_h2, \
                 nc.sbuf_tensor("l1_st", [128, 2 * 16], F32) as st_h:
                xb, lb, st = xb_h.ap(), lb_h.ap(), st_h.ap()
                self.ln_xc, self.ln_sq = xc_h.ap(), sq_h2.ap()
                for ti in range(NT):
                    p = ti % 2
                    self.emit_ln(rt_tiles[ti, :, :], lnT1, ti * 128,
                                 xb[:, p * D:(p + 1) * D], lb[:, p * D:(p + 1) * D],
                                 st[:, p * 16:(p + 1) * 16])
            self.emit_allgather_ln(lnT1, "cc_ln1_src", "cc_ln1_dst")
            ngrp = 0 if "l1ln" in self.stages else (1 if "l1g1" in self.stages else 2)
            for grp in range(ngrp):
                self.l1_group(grp, lnT1)
        self.barrier()
        self.tap("y2t", self.dram["y2t_hbm"].ap()[:, :, :])
        if "l1ln" in self.stages or "l1g1" in self.stages or "l1nop" in self.stages:
            return
        with nc.sbuf_tensor("wpn", [128, 8, D], BF16) as wpn_h, \
             nc.sbuf_tensor("wpT", [128, 8, D], BF16) as wpT_h, \
             nc.sbuf_tensor("y2s", [128, 2, 8, 128], BF16) as y2s_h, \
             nc.sbuf_tensor("rt1", [128, 2 * D], F32) as rt1_h:
            wpn, wpT, y2s, rt1 = wpn_h.ap(), wpT_h.ap(), y2s_h.ap(), rt1_h.ap()
            src = self.in_["wproj"].rearrange("(c p) f -> p c f", p=128)
            gn = self.dmaG(wpn[:, :, :], src[:, :, :])
            t.wait_ge(self.sG.h, gn)
            ev = self.sV.n
            for fc in range(8):
                for oc in range(8):
                    bank = self.pb[6 + (oc % 2)]
                    pv = bank.ap()[:, 0:64].bitcast(BF16)
                    t.wait_ge(self.sV.h, ev)
                    i = t.transpose(pv[:, 0:128], wpn[:, oc, fc * 128:(fc + 1) * 128],
                                    self.ident_b[:, :])
                    self.sT.inc(i)
                    v.wait_ge(self.sT.h, self.sT.n)
                    i = v.tensor_copy(wpT[:, fc, oc * 128:(oc + 1) * 128], pv[:, 0:128])
                    ev = self.sV.inc(i)
            self.barrier()
            y2_tiles = self.dram["y2t_hbm"].ap()
            evac_snap = [0, 0]
            store_snap = [0, 0]
            for ti in range(NT):
                pg = ti % 2
                sy.wait_ge(self.sD.h, store_snap[pg])
                dn = self.dmaS(y2s[:, pg, :, :],
                               y2_tiles[:, :, ti * 128:(ti + 1) * 128]
                               .rearrange("c p k -> p c k"))
                b0 = self.pb[pg * 2]
                b1 = self.pb[pg * 2 + 1]
                t.wait_ge(self.sD.h, dn)
                t.wait_ge(self.sV.h, self.sV.n if ti < 2 else evac_snap[pg])
                for fc in range(8):
                    st0 = (fc == 0)
                    sp = (fc == 7)
                    t.matmul(b0.ap()[:, 0:512], y2s[:, pg, fc, :], wpT[:, fc, 0:512],
                             start=st0, stop=sp)
                    i = t.matmul(b1.ap()[:, 0:512], y2s[:, pg, fc, :], wpT[:, fc, 512:1024],
                                 start=st0, stop=sp)
                self.sT.inc(i)
                tsnap = self.sT.n
                dn = self.dmaS(rt1[:, pg * D:pg * D + D], rt_tiles[ti, :, :])
                v.wait_ge(self.sD.h, dn)
                v.wait_ge(self.sT.h, tsnap)
                v.tensor_add(rt1[:, pg * D:pg * D + 512], b0.ap()[:, 0:512],
                             rt1[:, pg * D:pg * D + 512])
                i = v.tensor_add(rt1[:, pg * D + 512:pg * D + 1024], b1.ap()[:, 0:512],
                                 rt1[:, pg * D + 512:pg * D + 1024])
                evac_snap[pg] = self.sV.inc(i)
                sy.wait_ge(self.sV.h, evac_snap[pg])
                store_snap[pg] = self.dmaS(rt_tiles[ti, :, :], rt1[:, pg * D:pg * D + D])
            sy.wait_ge(self.sD.h, self.sD.n)
        self.barrier()
        self.tap("resid3", self.dram["resid_hbm"].ap()[:, :, :])

    def l1_group(self, grp, lnT1):
        nc = self.nc
        v, s, t, sy, g = nc.vector, nc.scalar, nc.tensor, nc.sync, nc.gpsimd
        with nc.sbuf_tensor(f"wTg{grp}", [128, DCH, 512], BF16) as qw_h, \
             nc.sbuf_tensor(f"wnl{grp}", [128, 2, D], BF16) as wn_h, \
             nc.sbuf_tensor(f"qhT{grp}", [128, 4, TOK], BF16) as qh_h, \
             nc.sbuf_tensor(f"khT{grp}", [128, 4, NSEQ], BF16) as kh_h, \
             nc.sbuf_tensor(f"vAg{grp}", [128, 32, 8 * 65], BF16) as vg_h, \
             nc.sbuf_tensor(f"tk{grp}", [128, 512], F32) as tk_h, \
             nc.sbuf_tensor(f"tk2{grp}", [128, 512], F32) as tk2_h, \
             nc.sbuf_tensor(f"tkb{grp}", [128, 512], BF16) as tkb_h, \
             nc.sbuf_tensor(f"trm{grp}", [128, 20], F32) as trm_h, \
             nc.sbuf_tensor(f"Pb1{grp}", [128, 2, 1024], BF16) as pb1_h, \
             nc.sbuf_tensor(f"otb{grp}", [65, 1024], F32) as otb_h, \
             nc.sbuf_tensor(f"y2g{grp}", [128, NT, 512], BF16) as y2g_h:
            wTg, wnl = qw_h.ap(), wn_h.ap()
            qhT, khT, vAg = qh_h.ap(), kh_h.ap(), vg_h.ap()
            tk, tk2, tkb, trm = tk_h.ap(), tk2_h.ap(), tkb_h.ap(), trm_h.ap()
            Pb1, otb, y2g = pb1_h.ap(), otb_h.ap(), y2g_h.ap()
            wsrc = self.in_["wqkv"].rearrange("(c p) d -> c p d", p=128)

            def load_wslice(wi):
                base = wi * 8 + grp * 4
                for hh in range(2):
                    t.wait_ge(self.sV.h, self.sV.n)
                    g.wait_ge(self.sT.h, self.sT.n)
                    gn = self.dmaG(wnl[:, :, :],
                                   wsrc[base + hh * 2:base + hh * 2 + 2, :, :]
                                   .rearrange("c p d -> p c d"))
                    t.wait_ge(self.sG.h, gn)
                    ev = self.sV.n
                    for fc in range(2):
                        for dch in range(DCH):
                            bank = self.pb[6 + (dch % 2)]
                            pv = bank.ap()[:, 0:64].bitcast(BF16)
                            t.wait_ge(self.sV.h, ev)
                            i = t.transpose(pv[:, 0:128],
                                            wnl[:, fc, dch * 128:(dch + 1) * 128],
                                            self.ident_b[:, :])
                            self.sT.inc(i)
                            v.wait_ge(self.sT.h, self.sT.n)
                            i = v.tensor_copy(
                                wTg[:, dch, (hh * 2 + fc) * 128:(hh * 2 + fc) * 128 + 128],
                                pv[:, 0:128])
                            ev = self.sV.inc(i)

            if "l1w" in self.stages:
                return
            for which, wi_, dst, ntile in (("q", 0, qhT, NT), ("k", 1, khT, 32)):
                wT = wTg
                t.wait_ge(self.sV.h, self.sV.n)
                self.barrier()
                load_wslice(wi_)
                for ti in range(ntile):
                    bank = self.pb[4]
                    t.wait_ge(self.sV.h, self.sV.n)
                    for dch in range(DCH):
                        i = t.matmul(bank.ap()[:, 0:512],
                                     lnT1[:, dch, ti * 128:(ti + 1) * 128],
                                     wT[:, dch, 0:512],
                                     start=(dch == 0), stop=(dch == DCH - 1))
                    self.sT.inc(i)
                    v.wait_ge(self.sT.h, self.sT.n)
                    self.vs(v.tensor_copy(tk[:, :], bank.ap()[:, 0:512]))
                    self.vs(v.tensor_mul(tk2[:, :], tk[:, :], tk[:, :]))
                    tv = tk2[:, :].rearrange("p (h e) -> p h e", e=64)
                    self.vs(v.tensor_reduce(trm[:, 0:8], tv, axis=AX.X, op=ALU.add))
                    s.wait_ge(self.sV.h, self.sV.n)
                    i = s.activation(trm[:, 8:16], trm[:, 0:8], AF.Sqrt,
                                     bias=self.consts[:, 2:3], scale=1.0 / 64.0)
                    self.sS.inc(i)
                    v.wait_ge(self.sS.h, self.sS.n)
                    self.vs(v.reciprocal(trm[:, 0:8], trm[:, 8:16]))
                    rb = trm[:, 0:8].rearrange("p (h o) -> p h o", o=1).broadcast_to(
                        [128, 8, 64])
                    self.vs(v.tensor_tensor(tkb[:, :].rearrange("p (h e) -> p h e", e=64),
                                            tv if False else
                                            tk[:, :].rearrange("p (h e) -> p h e", e=64),
                                            rb, ALU.mult))
                    ev3 = self.sV.n
                    for fc in range(4):
                        bank2 = self.pb[6 + (fc % 2)]
                        pv = bank2.ap()[:, 0:64].bitcast(BF16)
                        t.wait_ge(self.sV.h, ev3)
                        i = t.transpose(pv[:, 0:128], tkb[:, fc * 128:(fc + 1) * 128],
                                        self.ident_b[:, :])
                        self.sT.inc(i)
                        v.wait_ge(self.sT.h, self.sT.n)
                        i = v.tensor_copy(dst[:, fc, ti * 128:(ti + 1) * 128], pv[:, 0:128])
                        ev3 = self.vs(i)
            if "l1qk" in self.stages:
                return
            self.barrier()
            load_wslice(2)
            vwT = wTg
            for ti in range(32):
                bank = self.pb[4]
                t.wait_ge(self.sV.h, self.sV.n)
                for dch in range(DCH):
                    i = t.matmul(bank.ap()[:, 0:512],
                                 lnT1[:, dch, ti * 128:(ti + 1) * 128],
                                 vwT[:, dch, 0:512],
                                 start=(dch == 0), stop=(dch == DCH - 1))
                self.sT.inc(i)
                v.wait_ge(self.sT.h, self.sT.n)
                src3 = bank.ap()[:, 0:512].rearrange("p (h e) -> p h e", e=64)
                dst3 = vAg[:, ti, 0:520].rearrange("p (h e) -> p h e", e=65)[:, :, 0:64]
                i = v.tensor_copy(dst3, src3)
                self.vs(i)
            i = v.memset(vAg[:, :, :].rearrange("p t (h e) -> p t h e", e=65)[:, :, :, 64:65],
                         1.0)
            self.vs(i)
            if grp == 0:
                self.tap("qhT", qhT[:, :, :])
                self.tap("khT", khT[:, :, :])
                self.tap("vAg", vAg[:, :, :])
            if "l1v" in self.stages:
                return
            self.barrier()
            nhl = 1 if "l1h1" in self.stages else 8
            for hl in range(nhl):
                pbase = (hl * 64) % 128
                fch = (hl * 64) // 128
                t.wait_ge(self.sV.h, self.sV.n)
                s.wait_ge(self.sV.h, self.sV.n)
                exp_done = {}
                NU = 64
                for u in range(NU + 1):
                    if u < NU:
                        kt, half = divmod(u, 2)
                        sb = self.pb[u % 2 * 2:u % 2 * 2 + 2]
                        if u >= 2:
                            t.wait_ge(self.sS.h, exp_done[u - 2])
                        for qq in range(2):
                            i = t.matmul(
                                sb[qq].ap()[:, 0:512],
                                khT[pbase:pbase + 64, fch, kt * 128:(kt + 1) * 128],
                                qhT[pbase:pbase + 64, fch,
                                    half * 1024 + qq * 512:half * 1024 + qq * 512 + 512],
                                start=True, stop=True)
                        tsnap = self.sT.inc(i)
                        s.wait_ge(self.sT.h, tsnap)
                        s.activation(Pb1[:, u % 2, 0:512], sb[0].ap()[:, 0:512],
                                     AF.Exp, bias=self.consts[:, 6:7], scale=SC1)
                        i = s.activation(Pb1[:, u % 2, 512:1024], sb[1].ap()[:, 0:512],
                                         AF.Exp, bias=self.consts[:, 6:7], scale=SC1)
                        exp_done[u] = self.sS.inc(i)
                    if u > 0:
                        kt0, half0 = divmod(u - 1, 2)
                        t.wait_ge(self.sS.h, exp_done[u - 1])
                        for qq in range(2):
                            i = t.matmul(
                                self.pb[4 + half0 * 2 + qq].ap()[0:65, 0:512],
                                vAg[:, kt0, hl * 65:(hl + 1) * 65],
                                Pb1[:, (u - 1) % 2, qq * 512:(qq + 1) * 512],
                                start=(kt0 == 0), stop=(kt0 == 31))
                        self.sT.inc(i)
                v.wait_ge(self.sT.h, self.sT.n)
                for halfq in range(2):
                  if True:
                    for j in range(2):
                        i = v.tensor_copy(otb[0:65, j * 512:(j + 1) * 512],
                                          self.pb[4 + halfq * 2 + j].ap()[0:65, 0:512])
                    self.vs(i)
                    t.wait_ge(self.sV.h, self.sV.n)
                    for ti in range(halfq * 8, halfq * 8 + 8):
                        tloc = ti - halfq * 8
                        bank = self.pb[(ti % 2)]
                        t.wait_ge(self.sV.h, self.sV.n)
                        i = t.transpose(bank.ap()[:, 0:65],
                                        otb[0:65, tloc * 128:(tloc + 1) * 128],
                                        self.ident_f[0:65, 0:65])
                        ts_ = self.sT.inc(i)
                        v.wait_ge(self.sT.h, ts_)
                        ot = bank.ap()[:, 0:65]
                        vown = vAg[:, ti, hl * 65:hl * 65 + 64]
                        self.vs(v.reciprocal(trm[:, 0:1], ot[:, 64:65]))
                        self.vs(v.tensor_scalar(tk[:, 0:64], ot[:, 0:64], trm[:, 0:1],
                                                None, ALU.mult))
                        self.vs(v.tensor_tensor(tk[:, 64:128], vown, vown, ALU.mult))
                        self.vs(v.tensor_reduce(trm[:, 1:2], tk[:, 64:128], axis=AX.X,
                                                op=ALU.add))
                        s.wait_ge(self.sV.h, self.sV.n)
                        i = s.activation(trm[:, 2:3], trm[:, 1:2], AF.Sqrt,
                                         bias=self.consts[:, 3:4], scale=1.0)
                        self.sS.inc(i)
                        v.wait_ge(self.sS.h, self.sS.n)
                        self.vs(v.tensor_scalar(trm[:, 2:3], trm[:, 2:3], 1e-12, None,
                                                ALU.max))
                        self.vs(v.reciprocal(trm[:, 3:4], trm[:, 2:3]))
                        self.vs(v.tensor_scalar(tk[:, 128:192], vown, trm[:, 3:4], None,
                                                ALU.mult))
                        self.vs(v.tensor_tensor(tk[:, 192:256], tk[:, 0:64],
                                                tk[:, 128:192], ALU.mult))
                        self.vs(v.tensor_reduce(trm[:, 4:5], tk[:, 192:256], axis=AX.X,
                                                op=ALU.add))
                        self.vs(v.tensor_scalar(trm[:, 5:6], trm[:, 4:5], -1.0, None,
                                                ALU.mult))
                        i = v.scalar_tensor_tensor(
                            y2g[:, ti, hl * 64:(hl + 1) * 64], tk[:, 128:192],
                            trm[:, 5:6], tk[:, 0:64], ALU.mult, ALU.add)
                        self.vs(i)
            if "l1noy2" in self.stages:
                return
            for ti in range(NT):
                ev2 = None
                v.wait_ge(self.sD.h, self.sD.n)
                for fc in range(4):
                    bank = self.pb[6 + (fc % 2)]
                    pv = bank.ap()[:, 0:64].bitcast(BF16)
                    t.wait_ge(self.sV.h, self.sV.n)
                    i = t.transpose(pv[:, 0:128], y2g[:, ti, fc * 128:(fc + 1) * 128],
                                    self.ident_b[:, :])
                    self.sT.inc(i)
                    v.wait_ge(self.sT.h, self.sT.n)
                    i = v.tensor_copy(self.gtev[:, fc % 2, :], pv[:, 0:128])
                    ev2 = self.sV.inc(i)
                    sy.wait_ge(self.sV.h, ev2)
                    self.dmaS(self.dram["y2t_hbm"].ap()[4 * grp + fc, :,
                                                        ti * 128:(ti + 1) * 128],
                              self.gtev[:, fc % 2, :])
                    v.wait_ge(self.sD.h, self.sD.n)

    def moe(self):
        nc = self.nc
        v, s, t, sy, g = nc.vector, nc.scalar, nc.tensor, nc.sync, nc.gpsimd
        rt_tiles = self.dram["resid_hbm"].ap()
        with nc.sbuf_tensor("lnTm", [128, DCH, TOK], BF16) as lnTm_h:
            lnTm = lnTm_h.ap()
            with nc.sbuf_tensor("m_xb", [128, 2 * D], F32) as xb_h, \
                 nc.sbuf_tensor("m_lb", [128, 2 * D], BF16) as lb_h, \
                 nc.sbuf_tensor("m_xc", [128, D], F32) as xc_h, \
                 nc.sbuf_tensor("m_sq", [128, D], F32) as sq_h2, \
                 nc.sbuf_tensor("m_st", [128, 2 * 16], F32) as st_h:
                xb, lb, st = xb_h.ap(), lb_h.ap(), st_h.ap()
                self.ln_xc, self.ln_sq = xc_h.ap(), sq_h2.ap()
                for ti in range(NT):
                    p = ti % 2
                    self.emit_ln(rt_tiles[ti, :, :], lnTm, ti * 128,
                                 xb[:, p * D:(p + 1) * D], lb[:, p * D:(p + 1) * D],
                                 st[:, p * 16:(p + 1) * 16])
            self.barrier()
            gn = self.dmaG(self.dram["cc_mln_src"].ap()[:, :], lnTm[:, :, 0:TOK])
            g.wait_ge(self.sG.h, gn)
            i = g.collective_compute("AllGather", ALU.bypass,
                                     replica_groups=[list(range(NCORES))],
                                     ins=[self.dram["cc_mln_src"].ap().opt()],
                                     outs=[self.dram["cc_mln_dst"].ap().opt()])
            self.sC.inc(i)
            g.wait_ge(self.sC.h, self.sC.n)
        with nc.sbuf_tensor("moeln", [128, DCH, 512], BF16) as moeln_h, \
             nc.sbuf_tensor("hT", [128, 64, 512], BF16) as hT_h, \
             nc.sbuf_tensor("w1c", [128, 2, DCH, 512], BF16) as w1c_h, \
             nc.sbuf_tensor("w2c", [128, 2, 8, D], BF16) as w2c_h, \
             nc.sbuf_tensor("sew", [128, 4], F32) as sew_h, \
             nc.sbuf_tensor("ybuf", [128, 4 * D], F32) as ybuf_h:
            moeln, hT, w1c, w2c = moeln_h.ap(), hT_h.ap(), w1c_h.ap(), w2c_h.ap()
            sew, ybuf = sew_h.ap(), ybuf_h.ap()
            pid2 = nc.partition_id(engines=[mybir.EngineType.Pool])
            dstv = self.dram["cc_mln_dst"].ap().rearrange("r p (c k) -> r p c k", c=DCH)
            for slot in range(2):
                gn = self.dmaG(moeln[:, :, slot * 256:(slot + 1) * 256],
                               dstv[slot, :, :, bass.ds(pid2 * 256, 256)])
            g.wait_ge(self.sG.h, gn)
            with nc.sbuf_tensor("ew", [128, 8], F32) as ew_h:
                ew = ew_h.ap()
                esrc = self.in_["expw"].rearrange("(q p) k -> p q k", p=128)
                dn = self.dmaS(ew[:, 0:8], esrc[:, :, :])
                v.wait_ge(self.sD.h, dn)
                i = v.tensor_tensor(sew[:, 0:4], ew[:, 0:8:2], ew[:, 1:8:2], ALU.add)
                self.vs(i)
            self.barrier()
            w1src = self.in_["w1"].rearrange("(c p) i -> p c i", p=128)
            wload = {0: self.dmaG(w1c[:, 0, :, :], w1src[:, :, 0:512])}
            for mc in range(16):
                if mc + 1 < 16:
                    if mc >= 1:
                        g.wait_ge(self.sT.h, self.mc_done[mc - 1])
                    wload[mc + 1] = self.dmaG(
                        w1c[:, (mc + 1) % 2, :, :],
                        w1src[:, :, (mc + 1) * 512:(mc + 2) * 512])
                t.wait_ge(self.sG.h, wload[mc])
                for ic4 in range(4):
                    bank = self.pb[ic4 % 2]
                    for dch in range(DCH):
                        i = t.matmul(bank.ap()[:, 0:512],
                                     w1c[:, mc % 2, dch, ic4 * 128:(ic4 + 1) * 128],
                                     moeln[:, dch, 0:512],
                                     start=(dch == 0), stop=(dch == DCH - 1))
                    tsnap = self.sT.inc(i)
                    s.wait_ge(self.sT.h, tsnap)
                    i = s.activation(hT[:, mc * 4 + ic4, :], bank.ap()[:, 0:512],
                                     AF.Gelu_apprx_tanh)
                    ssnap = self.sS.inc(i)
                    t.wait_ge(self.sS.h, ssnap)
                self.mc_done = getattr(self, "mc_done", {})
                self.mc_done[mc] = self.sT.n
            self.barrier()
            w2src = self.in_["w2"].rearrange("(c p) d -> p c d", p=128)
            w2load = {0: self.dmaG(w2c[:, 0, :, :], w2src[:, 0:8, :])}
            for wb in range(8):
                if wb + 1 < 8:
                    if wb >= 1:
                        g.wait_ge(self.sT.h, self.wb_done[wb - 1])
                    w2load[wb + 1] = self.dmaG(w2c[:, (wb + 1) % 2, :, :],
                                               w2src[:, (wb + 1) * 8:(wb + 2) * 8, :])
                t.wait_ge(self.sG.h, w2load[wb])
                for ic8 in range(8):
                    ic = wb * 8 + ic8
                    for sub in range(4):
                        for half in range(2):
                            i = t.matmul(self.pb[sub * 2 + half].ap()[:, 0:512],
                                         hT[:, ic, sub * 128:(sub + 1) * 128],
                                         w2c[:, wb % 2, ic8, half * 512:(half + 1) * 512],
                                         start=(ic == 0), stop=(ic == 63))
                self.wb_done = getattr(self, "wb_done", {})
                self.wb_done[wb] = self.sT.inc(i)
            v.wait_ge(self.sT.h, self.sT.n)
            for sub in range(4):
                v.tensor_scalar(ybuf[:, sub * D:sub * D + 512],
                                self.pb[sub * 2].ap()[:, 0:512], sew[:, sub:sub + 1],
                                None, ALU.mult)
                i = v.tensor_scalar(ybuf[:, sub * D + 512:sub * D + 1024],
                                    self.pb[sub * 2 + 1].ap()[:, 0:512],
                                    sew[:, sub:sub + 1], None, ALU.mult)
            vsn = self.sV.inc(i)
            sy.wait_ge(self.sV.h, vsn)
            self.dmaS(self.dram["cc_mret_src"].ap().rearrange("(q p) d -> p q d", p=128),
                      ybuf[:, :].rearrange("p (q d) -> p q d", d=D))
        self.barrier()
        i = g.collective_compute("AllGather", ALU.bypass,
                                 replica_groups=[list(range(NCORES))],
                                 ins=[self.dram["cc_mret_src"].ap().opt()],
                                 outs=[self.dram["cc_mret_dst"].ap().opt()])
        self.sC.inc(i)
        g.wait_ge(self.sC.h, self.sC.n)
        self.barrier()
        with nc.sbuf_tensor("mret", [128, 2, D], F32) as mret_h, \
             nc.sbuf_tensor("rtf", [128, 2, D], F32) as rtf_h:
            mret, rtf = mret_h.ap(), rtf_h.ap()
            pid3 = nc.partition_id(engines=[mybir.EngineType.SP])
            off = nc.s_assert_within((pid3 % 2) * 256, min_val=0, max_val=256,
                                     skip_runtime_assert=True)
            retv = self.dram["cc_mret_dst"].ap()
            gsn = {0: 0, 1: 0}
            for ti in range(NT):
                p = ti % 2
                sy.wait_ge(self.sV.h, self.sV.n)
                self.dmaS(mret[:, p, :],
                          retv[ti // 2, bass.ds(off + (ti % 2) * 128, 128), :])
                dn = self.dmaS(rtf[:, p, :], rt_tiles[ti, :, :])
                v.wait_ge(self.sD.h, dn)
                i = v.scalar_tensor_tensor(rtf[:, p, :], mret[:, p, :],
                                           self.moe_gate_col[:, 0:1], rtf[:, p, :],
                                           ALU.mult, ALU.add)
                vsn2 = self.sV.inc(i)
                sy.wait_ge(self.sV.h, vsn2)
                self.dmaS(self.out[ti, :, :], rtf[:, p, :])
        self.barrier()

    def build(self, stages=("l0a", "l0f", "l1a", "moe")):
        self.stages = stages
        nc = self.nc
        with nc.Block():
            self.prologue()
            self.gtev = nc.alloc_sbuf_tensor("gtev", [128, 2, 128], BF16).ap()
            if "l0a" in self.stages:
                self.l0_attn()
            if "l0f" in self.stages:
                self.l0_ffn()
            if "l1a" in self.stages:
                self.l1_attn()
            if "moe" in self.stages:
                self.moe()
            sy = nc.sync
            self.wait_all(sy)
            if "moe" not in self.stages:
                for ti in range(NT):
                    self.dmaS(self.out[ti, :, :], self.dram["resid_hbm"].ap()[ti, :, :])
            sy.wait_ge(self.sD.h, self.sD.n)
        return nc


def _ownfirst(c):
    p = c % 2
    own = np.concatenate([np.arange(256 * (2 * j + p), 256 * (2 * j + p) + 256)
                          for j in range(8)])
    peer = np.concatenate([np.arange(256 * (2 * j + 1 - p), 256 * (2 * j + 1 - p) + 256)
                           for j in range(8)])
    return own, peer


def _build_masks(c):
    p = c % 2
    m = np.ones((8, 4, 128, 256), np.float32)
    for qc in range(8):
        qa = 256 * (2 * qc + p) + np.arange(256)
        for i in range(4):
            rng, blk = divmod(i, 2)
            blk += 2 * qc
            if rng == 0:
                kb = 256 * (2 * (blk // 2) + p) + (blk % 2) * 128
            else:
                kb = 256 * (2 * (blk // 2) + 1 - p) + (blk % 2) * 128
            ka = kb + np.arange(128)
            m[qc, i] = (ka[:, None] <= qa[None, :]).astype(np.float32)
    return np.ascontiguousarray(
        m.transpose(2, 0, 1, 3).reshape(128, 8 * 4 * 256)).astype(BFnp)


_NC_CACHE = {}


def _get_nc(taps=(), stages=("l0a", "l0f", "l1a", "moe")):
    key = (tuple(sorted(taps)), tuple(stages))
    if key not in _NC_CACHE:
        _NC_CACHE[key] = KB(debug_taps=taps).build(stages)
    return _NC_CACHE[key]


def kernel(**inputs):
    return _run(inputs, taps=())[0]


def _run(inputs, taps=(), stages=("l0a", "l0f", "l1a", "moe")):
    nc = _get_nc(taps, stages)
    X = np.ascontiguousarray(np.asarray(inputs["L0_attn_X"], dtype=np.float32))
    base = {
        "wq": np.ascontiguousarray(inputs["L0_attn_W_q"], dtype=np.float32),
        "wk": np.ascontiguousarray(inputs["L0_attn_W_k"], dtype=np.float32),
        "wv": np.ascontiguousarray(inputs["L0_attn_W_v"], dtype=np.float32),
        "wo": np.ascontiguousarray(inputs["L0_attn_W_o"], dtype=np.float32),
        "lq1": np.ascontiguousarray(inputs["L0_attn_lambda_q1"], dtype=np.float32),
        "lk1": np.ascontiguousarray(inputs["L0_attn_lambda_k1"], dtype=np.float32),
        "lq2": np.ascontiguousarray(inputs["L0_attn_lambda_q2"], dtype=np.float32),
        "lk2": np.ascontiguousarray(inputs["L0_attn_lambda_k2"], dtype=np.float32),
        "rms_scale": np.ascontiguousarray(inputs["L0_attn_rms_scale"], dtype=np.float32),
        "wgate": np.ascontiguousarray(inputs["L0_ffn_w_gate"], dtype=np.float32),
        "wup": np.ascontiguousarray(inputs["L0_ffn_w_up"], dtype=np.float32),
        "wdown": np.ascontiguousarray(inputs["L0_ffn_w_down"], dtype=np.float32),
        "wqkv": np.ascontiguousarray(inputs["L1_attn_w_qkv"], dtype=np.float32),
        "wproj": np.ascontiguousarray(inputs["L1_attn_w_proj"], dtype=np.float32),
        "w1": np.ascontiguousarray(inputs["L1_ffn_w1"], dtype=np.float32),
        "w2": np.ascontiguousarray(inputs["L1_ffn_w2"], dtype=np.float32),
    }
    expw = np.asarray(inputs["L1_ffn_expert_weights"], dtype=np.float32)
    in_maps = []
    for c in range(NCORES):
        own, _ = _ownfirst(c)
        m = dict(base)
        m["x_own"] = np.ascontiguousarray(X[c // 2][own])
        m["expw"] = np.ascontiguousarray(expw[512 * c:512 * (c + 1)])
        m["masks"] = _build_masks(c)
        m["moe_gate"] = np.full((128, 1), 1.0 if c < 2 else 0.0, np.float32)
        in_maps.append(m)
    res = run_bass_kernel_spmd(nc, in_maps, core_ids=list(range(NCORES)))
    out = np.zeros((4, NSEQ, D), np.float32)
    for c in range(NCORES):
        own, _ = _ownfirst(c)
        out[c // 2][own] = res.results[c]["out"].reshape(TOK, D)
    return out, res
